# version 19
# baseline (speedup 1.0000x reference)
import numpy as np
import ml_dtypes
import concourse.bass as bass
import concourse.mybir as mybir
from concourse.bass_utils import run_bass_kernel_spmd

F32 = mybir.dt.float32
BF16 = mybir.dt.bfloat16
AF = mybir.ActivationFunctionType
ALU = mybir.AluOpType
AX = mybir.AxisListType

NCORE = 8
TC = 1024
NT = 8
C = 2048
NCC = 16
H = 16
NHC = 44
HF = 5632
NW = HF * C
EPS_LN = 1e-5
EPS_Q = 1e-8
EPS_Z = 1e-6

GCOLS = NW // NCORE // 128
NGP = 8
GP = GCOLS // NGP

NPHASES = 5
USE_CC = 1
B4_NOC = 16
DEBUG_DUMP = 0

KBUF = 0
VST = 16384
KVBF = 20480


class Mil:
    def __init__(self):
        self.c = {}
        self.m = {}

    def bump(self, sem, key=None, k=1):
        self.c[sem] = self.c.get(sem, 0) + k
        if key is not None:
            kk = (sem, key)
            assert kk not in self.m, kk
            self.m[kk] = self.c[sem]
        return self.c[sem]

    def at(self, sem, key):
        return self.m[(sem, key)]


class _DI:
    def then_inc(self, *a, **k):
        return self


class _DummyEng:
    def __getattr__(self, name):
        return lambda *a, **k: _DI()


SEM_NAMES = [
    "dgp", "dmisc", "dx0", "dx1", "dg0", "dg1", "dw0", "dw1",
    "dq0", "dq1", "ds0", "ds1", "do0", "do1", "sv", "sa", "spe", "cc",
    "rb0", "rb1", "rb2", "rb3", "tb0", "tb1", "kb0", "kb1",
]


def build_program(emit, nc=None, mil=None):
    if mil is None:
        mil = Mil()

    class B:
        pass

    b = B()
    b.emit = emit
    E = _DummyEng()

    def inc(instr, sem, key=None, k=1):
        if emit:
            instr.then_inc(b.sems[sem], k)
        else:
            mil.bump(sem, key, k)
        return instr

    def wge(eng, sem, key):
        if emit:
            eng.wait_ge(b.sems[sem], mil.at(sem, key))

    def wval(eng, sem, val):
        if emit:
            eng.wait_ge(b.sems[sem], val)

    io = {}
    if emit:
        b.sems = {n: nc.alloc_semaphore(n) for n in SEM_NAMES}

        io["dx"] = nc.dram_tensor("x", [TC, C], F32, kind="ExternalInput").ap()
        io["dxcs"] = nc.dram_tensor("xcs", [NCC, 128, 1024], F32,
                                    kind="ExternalInput").ap()
        io["dwq"] = nc.dram_tensor("wqS", [H, 128, 2048], F32,
                                   kind="ExternalInput").ap()
        io["dwk"] = nc.dram_tensor("wkS", [8, 128, 4096], F32,
                                   kind="ExternalInput").ap()
        io["dwv"] = nc.dram_tensor("wvS", [8, 128, 4096], F32,
                                   kind="ExternalInput").ap()
        io["dwo"] = nc.dram_tensor("woS", [NCC, 128, 2048], F32,
                                   kind="ExternalInput").ap()
        io["dw1"] = nc.dram_tensor("w1S", [NHC, 128, 2048], F32,
                                   kind="ExternalInput").ap()
        io["dw2"] = nc.dram_tensor("w2S", [NHC, 128, 2048], F32,
                                   kind="ExternalInput").ap()
        io["dw3"] = nc.dram_tensor("w3S", [NCC, 128, 5632], F32,
                                   kind="ExternalInput").ap()
        io["dg1"] = nc.dram_tensor("g1", [128, GCOLS], F32, kind="ExternalInput").ap()
        io["dg2"] = nc.dram_tensor("g2", [128, GCOLS], F32, kind="ExternalInput").ap()
        io["dg3"] = nc.dram_tensor("g3", [128, GCOLS], F32, kind="ExternalInput").ap()
        io["didb"] = nc.dram_tensor("idb", [128, 128], BF16, kind="ExternalInput").ap()
        io["dout"] = nc.dram_tensor("out", [TC, C], F32, kind="ExternalOutput").ap()
        if DEBUG_DUMP:
            io["dbg_tbc"] = nc.dram_tensor("dbg_tbc", [128, 8], F32,
                                           kind="ExternalOutput").ap()
            io["dbg_ks"] = nc.dram_tensor("dbg_ks", [128, 16], BF16,
                                          kind="ExternalOutput").ap()
            io["dbg_ap"] = nc.dram_tensor("dbg_ap", [128, 2048], BF16,
                                          kind="ExternalOutput").ap()
            io["dbg_hid"] = nc.dram_tensor("dbg_hid", [128, 2048], BF16,
                                           kind="ExternalOutput").ap()
            io["dbg_xnt"] = nc.dram_tensor("dbg_xnt", [128, 2048], BF16,
                                           kind="ExternalOutput").ap()
            io["dbg_kv"] = nc.dram_tensor("dbg_kv", [128, 2048], BF16,
                                          kind="ExternalOutput").ap()

        io["ccg_in"] = nc.dram_tensor("ccg_in", [1, 4], F32)
        io["ccg_out"] = nc.dram_tensor("ccg_out", [1, 4], F32)
        io["cck_in"] = nc.dram_tensor("cck_in", [128, 2064], F32)
        io["cck_out"] = nc.dram_tensor("cck_out", [128, 2064], F32)

        A = nc.alloc_sbuf_tensor
        io["xnT"] = A("xnT", [128, 16384], BF16)
        io["arena"] = A("arena", [128, NHC * 1024], BF16)
        io["attn_pre"] = A("attn_pre", [128, 16384], BF16)
        io["wsb"] = A("wsb", [128, 9728], BF16)
        io["wf32"] = A("wf32", [128, 4096], F32)
        io["qh"] = A("qh", [128, 2048], BF16)
        io["ev1"] = A("ev1", [128, 1024], BF16)
        io["ev2"] = A("ev2", [128, 1024], BF16)
        io["xsl"] = A("xsl", [128, 2048], F32)
        io["ksum_sb"] = A("ksum_sb", [128, 16], F32)
        io["ksum_bf"] = A("ksum_bf", [128, 16], BF16)
        io["lnst"] = A("lnst", [128, 8], F32)
        io["gacc"] = A("gacc", [128, 24], F32)
        io["gmat"] = A("gmat", [128, 4], F32)
        io["tsb"] = A("tsb", [1, 4], F32)
        io["gsum"] = A("gsum", [1, 4], F32)
        io["tbc"] = A("tbc", [128, 8], F32)
        io["idb"] = A("idb_sb", [128, 128], BF16)
        io["ones1b"] = A("ones1b", [1, 128], BF16)
        io["ones1f"] = A("ones1f", [1, 128], F32)
        io["onecf"] = A("onecf", [128, 1], F32)
        io["onecb"] = A("onecb", [128, 1], BF16)
        io["epsln"] = A("epsln", [128, 1], F32)

        P = nc.alloc_psum_tensor
        io["ps_r"] = [P(f"ps_r{i}", [128, 512], F32) for i in range(4)]
        io["ps_tb"] = [P("ps_tb0", [128, 1024], BF16),
                       P("ps_tb1", [128, 1024], BF16)]
        io["ps_kv"] = [P("ps_kv0", [128, 512], F32), P("ps_kv1", [128, 512], F32)]

    rb_use = [0, 0, 0, 0]
    tb_use = [0, 0]
    kb_use = [0, 0]
    U = {}

    def rot_wait(eng, k, key=None):
        if rb_use[k] > 0:
            wval(eng, f"rb{k}", rb_use[k])
        rb_use[k] += 1
        if key is not None:
            U[key] = (f"rb{k}", rb_use[k])
        return rb_use[k]

    def rot_done(instr, k):
        inc(instr, f"rb{k}")

    def uwait(eng, key):
        sem, val = U[key]
        wval(eng, sem, val)

    def tb_wait(eng, k, key=None):
        if tb_use[k] > 0:
            wval(eng, f"tb{k}", tb_use[k])
        tb_use[k] += 1
        if key is not None:
            U[key] = (f"tb{k}", tb_use[k])
        return tb_use[k]

    def tb_done(instr, k):
        inc(instr, f"tb{k}")

    def kb_wait(eng, k, key=None):
        if kb_use[k] > 0:
            wval(eng, f"kb{k}", kb_use[k])
        kb_use[k] += 1
        if key is not None:
            U[key] = (f"kb{k}", kb_use[k])
        return kb_use[k]

    def kb_done(instr, k):
        inc(instr, f"kb{k}")

    def B0_sync(sp):
        inc(sp.dma_start(out=io["idb"][:, :], in_=io["didb"]) if emit else E.x(),
            "dmisc", ("idb",), 16)
        for tc in range(NT):
            if tc >= 2:
                wge(sp, "sv", ("cast", tc - 2))
            ins = E.x()
            if emit:
                src = io["dx"][tc * 128:(tc + 1) * 128, :]
                dst = io["wf32"][:, (tc % 2) * 2048:(tc % 2) * 2048 + 2048]
                ins = sp.dma_start(out=dst, in_=src)
            inc(ins, f"dx{tc % 2}", ("x", tc), 16)

    def B0_gamma_sync(sp):
        for m in range(3):
            for p in range(NGP):
                k = m * NGP + p
                if k < 2:
                    wge(sp, "sv", ("cast", NT - 1))
                else:
                    wge(sp, "sv", ("gred", k - 2))
                ins = E.x()
                if emit:
                    g = [io["dg1"], io["dg2"], io["dg3"]][m]
                    dst = io["wf32"][:, (k % 2) * 2048:(k % 2) * 2048 + GP]
                    ins = sp.dma_start(out=dst, in_=g[:, p * GP:(p + 1) * GP])
                inc(ins, f"dg{k % 2}", ("gpiece", k), 16)

    def B0_gp_ccg(gp):
        uwait(gp, ("gmm_ev",))
        inc(gp.dma_start(out=io["ccg_in"][:, :], in_=io["tsb"][:, :])
            if emit else E.x(), "dgp", ("ccg_in",), 16)
        wge(gp, "dgp", ("ccg_in",))
        ins = E.x()
        if emit and USE_CC:
            ins = gp.collective_compute(
                "AllReduce", ALU.add, replica_groups=[list(range(NCORE))],
                ins=[io["ccg_in"].ap()], outs=[io["ccg_out"].ap()])
        elif emit:
            ins = gp.nop()
        inc(ins, "cc", ("arg",), 1)

    def B0_scalar(s):
        for tc in range(NT):
            wge(s, f"dx{tc % 2}", ("x", tc))
            ins = E.x()
            if emit:
                sx = io["wf32"][:, (tc % 2) * 2048:(tc % 2) * 2048 + 2048]
                ins = s.activation(io["xsl"][:, :], sx, AF.Square,
                                   accum_out=io["lnst"][:, 2:3])
            inc(ins, "sa", ("sq", tc))
            wge(s, "sv", ("var", tc))
            ins = E.x()
            if emit:
                ins = s.activation(io["lnst"][:, 5:6], io["lnst"][:, 4:5],
                                   AF.Ln, bias=io["epsln"][:, 0:1], scale=1.0)
            inc(ins, "sa", ("lnv", tc))
            wge(s, "sa", ("lnv", tc))
            ins = E.x()
            if emit:
                ins = s.activation(io["lnst"][:, 6:7], io["lnst"][:, 5:6],
                                   AF.Exp, scale=-0.5)
            inc(ins, "sa", ("rstd", tc))

    def B0_vector(v):
        ins = E.x()
        if emit:
            v.memset(io["ones1b"][:, :], 1.0)
            v.memset(io["ones1f"][:, :], 1.0)
            v.memset(io["onecf"][:, :], 1.0)
            v.memset(io["onecb"][:, :], 1.0)
            v.memset(io["tsb"][:, :], 0.0)
            v.memset(io["gmat"][:, :], 0.0)
            ins = v.memset(io["epsln"][:, :], EPS_LN)
        inc(ins, "sv", ("memset",))
        for tc in range(NT):
            wge(v, f"dx{tc % 2}", ("x", tc))
            ins = E.x()
            if emit:
                sx = io["wf32"][:, (tc % 2) * 2048:(tc % 2) * 2048 + 2048]
                ins = v.tensor_reduce(io["lnst"][:, 0:1], sx, AX.X, ALU.add)
            inc(ins, "sv", ("r1", tc))
            wge(v, "sv", ("r1", tc))
            ins = E.x()
            if emit:
                ins = v.tensor_scalar(io["lnst"][:, 1:2], io["lnst"][:, 0:1],
                                      -1.0 / C, None, ALU.mult)
            inc(ins, "sv", ("nm", tc))
            wge(v, "sa", ("sq", tc))
            wge(v, "sv", ("nm", tc))
            ins = E.x()
            if emit:
                ins = v.tensor_tensor(io["lnst"][:, 3:4], io["lnst"][:, 1:2],
                                      io["lnst"][:, 1:2], ALU.mult)
            inc(ins, "sv", ("m2", tc))
            wge(v, "sv", ("m2", tc))
            ins = E.x()
            if emit:
                ins = v.scalar_tensor_tensor(io["lnst"][:, 4:5],
                                             io["lnst"][:, 2:3], 1.0 / C,
                                             io["lnst"][:, 3:4],
                                             ALU.mult, ALU.subtract)
            inc(ins, "sv", ("var", tc))
            wge(v, "sa", ("rstd", tc))
            ins = E.x()
            if emit:
                ins = v.tensor_tensor(io["lnst"][:, 7:8], io["lnst"][:, 1:2],
                                      io["lnst"][:, 6:7], ALU.mult)
            inc(ins, "sv", ("bb", tc))
            wge(v, "sv", ("bb", tc))
            ins = E.x()
            if emit:
                xn = io["attn_pre"][:, 0:2048]
                ins = v.tensor_scalar(xn, sx, io["lnst"][:, 6:7],
                                      io["lnst"][:, 7:8], ALU.mult, ALU.add)
            inc(ins, "sv", ("cast", tc))
            for r in range(2):
                wge(v, "spe", ("xtr", tc, r))
                ins = E.x()
                if emit:
                    dst = io["xnT"].ap().rearrange("p (cc t) -> p cc t", cc=NCC)
                    dst = dst[:, r * 8:(r + 1) * 8, tc * 128:(tc + 1) * 128]
                    src = io["ps_tb"][r].ap().rearrange("p (z t) -> p z t", z=8)
                    ins = v.tensor_copy(dst, src)
                tb_done(ins, r)
    def B0_tensor(t):
        wge(t, "dmisc", ("idb",))
        wge(t, "sv", ("memset",))
        for tc in range(NT):
            wge(t, "sv", ("cast", tc))
            for r in range(2):
                tb_wait(t, r, key=("xnt", tc, r))
                ins = E.x()
                for blk_i in range(8):
                    if emit:
                        cc = r * 8 + blk_i
                        src = io["attn_pre"][:, cc * 128:(cc + 1) * 128]
                        ins = t.transpose(
                            io["ps_tb"][r][:, blk_i * 128:(blk_i + 1) * 128],
                            src, io["idb"][:, :])
                inc(ins, "spe", ("xtr", tc, r))
            if tc < 4:
                if tc == 0:
                    wge(t, "dw0", ("kslab", 0))
                uwait(t, ("xnt", tc, 0))
                uwait(t, ("xnt", tc, 1))
                k = tc % 4
                rot_wait(t, k, key=("Kt", 0, tc))
                ins = E.x()
                for cc in range(NCC):
                    if emit:
                        lhsT = io["xnT"][:, cc * 1024 + tc * 128:
                                         cc * 1024 + (tc + 1) * 128]
                        rhs = wslab(0)[:, cc * 256:(cc + 1) * 256]
                        ins = t.matmul(io["ps_r"][k][:, 0:256], lhsT, rhs,
                                       start=(cc == 0), stop=(cc == NCC - 1))
                inc(ins, "spe", ("kproj", 0, tc))

    def wslab(i):
        return io["wsb"][:, (i % 2) * 4096:(i % 2) * 4096 + 4096]

    def B1_gp(gp):
        for part, dwn in (("k", "dwk"), ("v", "dwv")):
            for osl in range(8):
                if part == "k":
                    if osl >= 2:
                        wge(gp, "spe", ("kproj", osl - 2, NT - 1))
                else:
                    if osl <= 1:
                        wge(gp, "spe", ("kproj", 6 + osl, NT - 1))
                    else:
                        wge(gp, "spe", ("vproj", osl - 2, NT - 1))
                ins = E.x()
                if emit:
                    ins = gp.dma_start(out=wslab(osl), in_=io[dwn][osl])
                inc(ins, f"dw{osl % 2}", (part + "slab", osl), 16)

    def B1_tensor(t):
        for tc in range(4, NT):
            uwait(t, ("xnt", tc, 0))
            uwait(t, ("xnt", tc, 1))
            k = tc % 4
            rot_wait(t, k, key=("Kt", 0, tc))
            ins = E.x()
            for cc in range(NCC):
                if emit:
                    lhsT = io["xnT"][:, cc * 1024 + tc * 128:
                                     cc * 1024 + (tc + 1) * 128]
                    rhs = wslab(0)[:, cc * 256:(cc + 1) * 256]
                    ins = t.matmul(io["ps_r"][k][:, 0:256], lhsT, rhs,
                                   start=(cc == 0), stop=(cc == NCC - 1))
            inc(ins, "spe", ("kproj", 0, tc))
        tix = NT
        for osl in range(1, 8):
            wge(t, f"dw{osl % 2}", ("kslab", osl))
            for tc in range(NT):
                k = tix % 4
                rot_wait(t, k, key=("Kt", osl, tc))
                ins = E.x()
                for cc in range(NCC):
                    if emit:
                        lhsT = io["xnT"][:, cc * 1024 + tc * 128:
                                         cc * 1024 + (tc + 1) * 128]
                        rhs = wslab(osl)[:, cc * 256:(cc + 1) * 256]
                        ins = t.matmul(io["ps_r"][k][:, 0:256], lhsT, rhs,
                                       start=(cc == 0), stop=(cc == NCC - 1))
                inc(ins, "spe", ("kproj", osl, tc))
                tix += 1
        for osl in range(8):
            wge(t, f"dw{osl % 2}", ("vslab", osl))
            for tc in range(NT):
                k = tix % 4
                rot_wait(t, k, key=("Vt", osl, tc))
                ins = E.x()
                for cc in range(NCC):
                    if emit:
                        lhsT = io["xnT"][:, cc * 1024 + tc * 128:
                                         cc * 1024 + (tc + 1) * 128]
                        rhs = wslab(osl)[:, cc * 256:(cc + 1) * 256]
                        ins = t.matmul(io["ps_r"][k][:, 0:256], lhsT, rhs,
                                       start=(cc == 0), stop=(cc == NCC - 1))
                inc(ins, "spe", ("vproj", osl, tc))
                tix += 1
            if osl == 0:
                wge(t, "sv", ("gmat",))
                rot_wait(t, 0, key=("gmm_ev",))
                inc(t.matmul(io["ps_r"][0][0:1, 0:4], io["onecf"][:, 0:1],
                             io["gmat"][:, 0:4], start=True, stop=True)
                    if emit else E.x(), "spe", ("gmm",))
            uwait(t, ("Vt", osl, NT - 1))
            uwait(t, ("Kt", osl, NT - 1))
            g = osl % 2
            kb_wait(t, g, key=("kvg", osl))
            ins = E.x()
            first = True
            for hl in range(2):
                for tc in range(NT):
                    if emit:
                        base = KBUF + tc * 2048 + osl * 256 + hl * 128
                        kv_lhsT = io["arena"][:, base:base + 128]
                        vb = VST + g * 2048 + tc * 256 + hl * 128
                        vs = io["arena"][:, vb:vb + 128]
                        t.matmul(io["ps_kv"][g][:, hl * 128:(hl + 1) * 128],
                                 kv_lhsT, vs, start=first, stop=False)
                        ins = t.matmul(io["ps_kv"][g][:, 256 + hl:257 + hl],
                                       kv_lhsT, io["onecb"][:, 0:1],
                                       start=False,
                                       stop=(hl == 1 and tc == NT - 1))
                    first = False
            inc(ins, "spe", ("kvmm", osl))

    def B1_scalar(s):
        for osl in range(8):
            for tc in range(NT):
                i = osl * NT + tc
                wge(s, "spe", ("kproj", osl, tc))
                if i >= 2:
                    io2 = ((i - 2) // NT, (i - 2) % NT)
                    wge(s, "sv", ("kmn", io2[0], io2[1]))
                k = i % 4
                ins = E.x()
                if emit:
                    e = io["ev1"][:, (i % 2) * 256:(i % 2) * 256 + 256]
                    ins = s.activation(e, io["ps_r"][k][:, 0:256], AF.Exp)
                inc(ins, "sa", ("kexp", osl, tc))
        for osl in range(8):
            for tc in range(NT):
                wge(s, "spe", ("vproj", osl, tc))
                k = (64 + osl * NT + tc) % 4
                ins = E.x()
                if emit:
                    vb = VST + (osl % 2) * 2048 + tc * 256
                    ins = s.copy(io["arena"][:, vb:vb + 256],
                                 io["ps_r"][k][:, 0:256])
                rot_done(ins, k)

    def B1_vector(v):
        for osl in range(8):
            for tc in range(NT):
                i = osl * NT + tc
                wge(v, "sa", ("kexp", osl, tc))
                k = i % 4
                ins = E.x()
                if emit:
                    e = io["ev1"][:, (i % 2) * 256:(i % 2) * 256 + 256]
                    t1 = io["ev1"][:, 512 + (i % 2) * 256:512 + (i % 2) * 256 + 256]
                    ins = v.tensor_single_scalar(t1, e, 1.0, ALU.min)
                inc(ins, "sv", ("kmn", osl, tc))
                wge(v, "sv", ("kmn", osl, tc))
                ins = E.x()
                if emit:
                    base = KBUF + tc * 2048 + osl * 256
                    ins = v.scalar_tensor_tensor(io["arena"][:, base:base + 256],
                                                 io["ps_r"][k][:, 0:256], 0.0,
                                                 t1, ALU.max, ALU.add)
                rot_done(ins, k)
            for kk in range(3):
                k2 = osl * 3 + kk
                wge(v, f"dg{k2 % 2}", ("gpiece", k2))
                ins = E.x()
                if emit:
                    sg = io["wf32"][:, (k2 % 2) * 2048:(k2 % 2) * 2048 + GP]
                    ins = v.tensor_reduce(io["gacc"][:, k2:k2 + 1], sg, AX.X,
                                          ALU.add, apply_absolute_value=True)
                inc(ins, "sv", ("gred", k2))
        wge(v, "sv", ("gred", 3 * NGP - 1))
        ins = E.x()
        if emit:
            for m in range(3):
                ins = v.tensor_reduce(io["gmat"][:, m:m + 1],
                                      io["gacc"][:, m * NGP:(m + 1) * NGP],
                                      AX.X, ALU.add)
        inc(ins, "sv", ("gmat",))
        wge(v, "spe", ("gmm",))
        rot_done(v.tensor_copy(io["tsb"][0:1, 0:4], io["ps_r"][0][0:1, 0:4])
                 if emit else E.x(), 0)
        for osl in range(8):
            wge(v, "spe", ("kvmm", osl))
            g = osl % 2
            ins = E.x()
            if emit:
                v.tensor_copy(io["wf32"][:, osl * 256:(osl + 1) * 256],
                              io["ps_kv"][g][:, 0:256])
                ins = v.tensor_copy(io["ksum_sb"][:, osl * 2:osl * 2 + 2],
                                    io["ps_kv"][g][:, 256:258])
            kb_done(ins, g)
        uwait(v, ("kvg", 7))
        inc(v.tensor_copy(io["wf32"][:, 2048:2064], io["ksum_sb"][:, :])
            if emit else E.x(), "sv", ("ksfix",))

    def B1_gp2(gp):
        wge(gp, "sv", ("ksfix",))
        inc(gp.dma_start(out=io["cck_in"][:, :], in_=io["wf32"][:, 0:2064])
            if emit else E.x(), "dgp", ("cck_in",), 16)
        wge(gp, "dgp", ("cck_in",))
        ins = E.x()
        if emit and USE_CC:
            ins = gp.collective_compute(
                "AllReduce", ALU.add, replica_groups=[list(range(NCORE))],
                ins=[io["cck_in"].ap()], outs=[io["cck_out"].ap()])
        elif emit:
            ins = gp.nop()
        inc(ins, "cc", ("ark",), 1)

    def wqslab(h):
        return io["wsb"][:, (h % 4) * 2048:(h % 4) * 2048 + 2048]

    def qsl(h, th):
        return io["arena"][:, h * 1024 + th * 512:h * 1024 + (th + 1) * 512]

    WQSEM = ["dw0", "dw1", "dq0", "dq1"]

    def B2_gpA(gp):
        for h in range(4):
            wge(gp, "spe", ("vproj", 7, NT - 1))
            ins = E.x()
            if emit:
                ins = gp.dma_start(out=wqslab(h), in_=io["dwq"][h])
            inc(ins, WQSEM[h % 4], ("wqslab", h), 16)

    def B2_gp(gp):
        for h in range(4, 6):
            wge(gp, "spe", ("qproj", h - 4, 1))
            ins = E.x()
            if emit:
                ins = gp.dma_start(out=wqslab(h), in_=io["dwq"][h])
            inc(ins, WQSEM[h % 4], ("wqslab", h), 16)
        wge(gp, "cc", ("arg",))
        inc(gp.dma_start(out=io["gsum"][:, :], in_=io["ccg_out"][:, :])
            if emit else E.x(), "dgp", ("gsum",), 16)
        for h in range(6, H):
            wge(gp, "spe", ("qproj", h - 4, 1))
            ins = E.x()
            if emit:
                ins = gp.dma_start(out=wqslab(h), in_=io["dwq"][h])
            inc(ins, WQSEM[h % 4], ("wqslab", h), 16)
        wge(gp, "cc", ("ark",))
        ins1 = E.x()
        ins2 = E.x()
        if emit:
            ins1 = gp.dma_start(out=io["arena"][:, KVBF:KVBF + 2048],
                                in_=io["cck_out"][:, 0:2048])
            ins2 = gp.dma_start(out=io["ksum_bf"][:, :],
                                in_=io["cck_out"][:, 2048:2064])
        inc(ins1, "dgp", None, 16)
        inc(ins2, "dgp", ("kvbf",), 16)

    def B2_tensor(t):
        for h in range(H):
            wge(t, WQSEM[h % 4], ("wqslab", h))
            for th in range(2):
                k = (h * 2 + th) % 4
                rot_wait(t, k, key=("Qt", h, th))
                ins = E.x()
                for cc in range(NCC):
                    if emit:
                        lhsT = wqslab(h)[:, cc * 128:(cc + 1) * 128]
                        rhs = io["xnT"][:, cc * 1024 + th * 512:
                                        cc * 1024 + (th + 1) * 512]
                        ins = t.matmul(io["ps_r"][k][:, :], lhsT, rhs,
                                       start=(cc == 0), stop=(cc == NCC - 1))
                inc(ins, "spe", ("qproj", h, th))
            if h == 8:
                wge(t, "sv", ("tcalc",))
                rot_wait(t, 0, key=("tbc_ev",))
                inc(t.matmul(io["ps_r"][0][:, 0:4], io["ones1f"][0:1, :],
                             io["tsb"][0:1, 0:4], start=True, stop=True)
                    if emit else E.x(), "spe", ("tbc",))
        wge(t, "dgp", ("kvbf",))
        for h in range(H):
            for th in range(2):
                uwait(t, ("Qt", h, th))
                kb_wait(t, th, key=("dent", h, th))
                ins = E.x()
                if emit:
                    ins = t.matmul(io["ps_kv"][th][0:1, :],
                                   io["ksum_bf"][:, h:h + 1], qsl(h, th),
                                   start=True, stop=True)
                inc(ins, "spe", ("den", h, th))
            for th in range(2):
                ka = (h * 2 + th) % 4
                rot_wait(t, ka, key=("att", h, th))
                ins = E.x()
                if emit:
                    ins = t.matmul(io["ps_r"][ka][:, :],
                                   io["arena"][:, KVBF + h * 128:
                                               KVBF + (h + 1) * 128],
                                   qsl(h, th), start=True, stop=True)
                inc(ins, "spe", ("attn", h, th))
            for th in range(2):
                wge(t, "sa", ("z", h, th))
                kz = (h * 2 + th + 2) % 4
                rot_wait(t, kz, key=("zbt", h, th))
                ins = E.x()
                if emit:
                    ins = t.matmul(io["ps_r"][kz][:, :], io["ones1f"][0:1, :],
                                   io["xsl"][0:1, th * 512:(th + 1) * 512],
                                   start=True, stop=True)
                inc(ins, "spe", ("zb", h, th))

    def B2_scalar(s):
        for h in range(H):
            for th in range(2):
                wge(s, "spe", ("qproj", h, th))
                if h >= 1:
                    wge(s, "sv", ("qmn", h - 1, th))
                k = (h * 2 + th) % 4
                ins = E.x()
                if emit:
                    e = io["ev1"][:, th * 512:(th + 1) * 512]
                    ins = s.activation(e, io["ps_r"][k][:, :], AF.Exp)
                inc(ins, "sa", ("qexp", h, th))
        for h in range(H):
            for th in range(2):
                uwait(s, ("dent", h, th))
                ins = E.x()
                if emit:
                    zs = io["xsl"][0:1, th * 512:(th + 1) * 512]
                    ins = s.activation(zs, zs, AF.Ln)
                inc(ins, "sa", ("lnz", h, th))
                wge(s, "sa", ("lnz", h, th))
                ins = E.x()
                if emit:
                    zs = io["xsl"][0:1, th * 512:(th + 1) * 512]
                    ins = s.activation(zs, zs, AF.Exp, scale=-1.0)
                inc(ins, "sa", ("z", h, th))
            for th in range(2):
                wge(s, "spe", ("attn", h, th))
                if h >= 1:
                    uwait(s, ("zbt", h - 1, th))
                ka = (h * 2 + th) % 4
                ins = E.x()
                if emit:
                    ins = s.copy(io["ev2"][:, th * 512:(th + 1) * 512],
                                 io["ps_r"][ka][:, :])
                rot_done(ins, ka)

    def B2_vector(v):
        for h in range(H):
            if h == 6:
                wge(v, "dgp", ("gsum",))
                inc(v.tensor_scalar(io["tsb"][0:1, 0:4], io["gsum"][0:1, 0:4],
                                    0.5 / NW, 0.5 * EPS_Q, ALU.mult, ALU.add)
                    if emit else E.x(), "sv", ("tcalc",))
            if h == 9:
                wge(v, "spe", ("tbc",))
                if emit:
                    rot_done(v.tensor_copy(io["tbc"][:, 0:4],
                                           io["ps_r"][0][:, 0:4]), 0)
                    ins = v.tensor_scalar(io["tbc"][:, 4:8],
                                          io["ps_r"][0][:, 0:4],
                                          -1.0, None, ALU.mult)
                else:
                    rot_done(E.x(), 0)
                    ins = E.x()
                inc(ins, "sv", ("tbcast",))
            for th in range(2):
                wge(v, "sa", ("qexp", h, th))
                k = (h * 2 + th) % 4
                ins = E.x()
                if emit:
                    e = io["ev1"][:, th * 512:(th + 1) * 512]
                    t1 = io["ev2"][:, th * 512:(th + 1) * 512]
                    ins = v.tensor_single_scalar(t1, e, 1.0, ALU.min)
                inc(ins, "sv", ("qmn", h, th))
                wge(v, "sv", ("qmn", h, th))
                ins = E.x()
                if emit:
                    t1 = io["ev2"][:, th * 512:(th + 1) * 512]
                    ins = v.scalar_tensor_tensor(qsl(h, th), io["ps_r"][k][:, :],
                                                 0.0, t1, ALU.max, ALU.add)
                rot_done(ins, k)
        for h in range(H):
            for th in range(2):
                wge(v, "spe", ("den", h, th))
                ins = E.x()
                if emit:
                    zs = io["xsl"][0:1, th * 512:(th + 1) * 512]
                    ins = v.tensor_single_scalar(zs, io["ps_kv"][th][0:1, :],
                                                 EPS_Z, ALU.add)
                kb_done(ins, th)
            for th in range(2):
                uwait(v, ("att", h, th))
                wge(v, "spe", ("zb", h, th))
                kz = (h * 2 + th + 2) % 4
                ins = E.x()
                if emit:
                    dst = io["attn_pre"][:, h * 1024 + th * 512:
                                         h * 1024 + (th + 1) * 512]
                    ins = v.tensor_tensor(dst,
                                          io["ev2"][:, th * 512:(th + 1) * 512],
                                          io["ps_r"][kz][:, :], ALU.mult)
                rot_done(ins, kz)

    def B3_sync(sp):
        wge(sp, "dgp", ("cck_in",))
        for j in range(NHC):
            for m in range(2):
                if j >= 1:
                    wge(sp, "sv", ("quant", j - 1, m))
                ins = E.x()
                if emit:
                    dw = io["dw1"] if m == 0 else io["dw2"]
                    dst = io["wf32"][:, m * 2048:(m + 1) * 2048]
                    ins = sp.dma_start(out=dst, in_=dw[j])
                inc(ins, f"ds{m}", ("wst", j, m), 16)

    def B3_vector(v):
        for j in range(NHC):
            for m in range(2):
                wge(v, f"ds{m}", ("wst", j, m))
                if j <= 1:
                    wge(v, "spe", ("qproj", H - 1, 1))
                if j >= 2:
                    wge(v, "spe", ("x12", j - 2, m, 1))
                ins = E.x()
                if emit:
                    wsrc = io["wf32"][:, m * 2048:(m + 1) * 2048]
                    qoff = (m * 2 + j % 2) * 2048
                    qdst = io["wsb"][:, qoff:qoff + 2048]
                    ins = v.tensor_scalar(qdst, wsrc, io["tbc"][:, 4 + m:5 + m],
                                          -1.0, ALU.is_lt, ALU.mult)
                inc(ins, "sv", ("q1", j, m))
                wge(v, "sv", ("q1", j, m))
                ins = E.x()
                if emit:
                    ins = v.scalar_tensor_tensor(qdst, wsrc,
                                                 io["tbc"][:, m:m + 1],
                                                 qdst, ALU.is_gt, ALU.add)
                inc(ins, "sv", ("quant", j, m))
            for th in range(2):
                uwait(v, ("x12t", j, 0, th))
                wge(v, "spe", ("x12", j, 1, th))
                k2 = (j * 4 + 2 + th) % 4
                ins = E.x()
                if emit:
                    s1 = io["ev1"][:, th * 512:(th + 1) * 512]
                    dst = io["arena"][:, j * 1024 + th * 512:
                                      j * 1024 + (th + 1) * 512]
                    ins = v.tensor_tensor(dst, s1, io["ps_r"][k2][:, :],
                                          ALU.mult)
                rot_done(ins, k2)

    def B3_scalar(s):
        for j in range(NHC):
            for th in range(2):
                wge(s, "spe", ("x12", j, 0, th))
                if j >= 1:
                    uwait(s, ("x12t", j - 1, 1, th))
                k1 = (j * 4 + th) % 4
                ins = E.x()
                if emit:
                    ins = s.activation(io["ev1"][:, th * 512:(th + 1) * 512],
                                       io["ps_r"][k1][:, :], AF.Sigmoid)
                rot_done(ins, k1)

    def B3_tensor(t):
        for j in range(NHC):
            for m in range(2):
                wge(t, "sv", ("quant", j, m))
                for th in range(2):
                    k = (j * 4 + m * 2 + th) % 4
                    rot_wait(t, k, key=("x12t", j, m, th))
                    ins = E.x()
                    for cc in range(NCC):
                        if emit:
                            qoff = (m * 2 + j % 2) * 2048
                            lhsT = io["wsb"][:, qoff + cc * 128:
                                             qoff + (cc + 1) * 128]
                            rhs = io["xnT"][:, cc * 1024 + th * 512:
                                            cc * 1024 + (th + 1) * 512]
                            ins = t.matmul(io["ps_r"][k][:, :], lhsT, rhs,
                                           start=(cc == 0),
                                           stop=(cc == NCC - 1))
                    inc(ins, "spe", ("x12", j, m, th))

    def B4_gp(gp):
        wge(gp, "spe", ("x12", NHC - 1, 1, 1))
        for oc in range(B4_NOC):
            if oc >= 2:
                wge(gp, "spe", ("wo_done", oc - 2))
            ins = E.x()
            if emit:
                dst = io["wsb"][:, (oc % 2) * 2048:(oc % 2 + 1) * 2048]
                ins = gp.dma_start(out=dst, in_=io["dwo"][oc])
            inc(ins, f"dw{oc % 2}", ("woslab", oc), 16)

    def B4_sync(sp):
        wge(sp, "sv", ("quant", NHC - 1, 1))
        wge(sp, "spe", ("zb", H - 1, 1))
        for oc in range(B4_NOC):
            if oc >= 2:
                wge(sp, f"do{oc % 2}", ("outdma", oc - 2))
            ins = E.x()
            if emit:
                dst = io["xsl"][:, (oc % 2) * 1024:(oc % 2 + 1) * 1024]
                ins = sp.dma_start(out=dst, in_=io["dxcs"][oc])
            inc(ins, f"dx{oc % 2}", ("xslab", oc), 16)
            for pi in range(4):
                gpi = oc * 4 + pi
                if gpi >= 2:
                    poc, ppi = divmod(gpi - 2, 4)
                    wge(sp, "sv", ("w3q", poc, ppi))
                ins = E.x()
                if emit:
                    dst = io["wf32"][:, (pi % 2) * 1408:(pi % 2) * 1408 + 1408]
                    ins = sp.dma_start(out=dst,
                                       in_=io["dw3"][oc][:, pi * 1408:(pi + 1) * 1408])
                inc(ins, f"ds{pi % 2}", ("w3st", oc, pi), 16)
            if oc >= 1:
                uwait(sp, ("otr", oc - 1, 0))
                uwait(sp, ("otr", oc - 1, 1))
                ins = E.x()
                if emit:
                    dst = io["dout"].rearrange("(tc p) c -> p tc c", p=128)
                    dst = dst[:, :, (oc - 1) * 128:oc * 128]
                    src = io["xsl"][:, ((oc - 1) % 2) * 1024:((oc - 1) % 2 + 1) * 1024]
                    src = src.rearrange("p (tc c) -> p tc c", tc=NT)
                    ins = sp.dma_start(out=dst, in_=src)
                inc(ins, f"do{(oc - 1) % 2}", ("outdma", oc - 1), 16)
        oc = B4_NOC - 1
        uwait(sp, ("otr", oc, 0))
        uwait(sp, ("otr", oc, 1))
        ins = E.x()
        if emit:
            dst = io["dout"].rearrange("(tc p) c -> p tc c", p=128)
            dst = dst[:, :, oc * 128:(oc + 1) * 128]
            src = io["xsl"][:, (oc % 2) * 1024:(oc % 2 + 1) * 1024]
            src = src.rearrange("p (tc c) -> p tc c", tc=NT)
            ins = sp.dma_start(out=dst, in_=src)
        inc(ins, f"do{oc % 2}", ("outdma", oc), 16)
        last_even = (B4_NOC - 1) // 2 * 2
        last_odd = (B4_NOC - 2) // 2 * 2 + 1
        wge(sp, "do0", ("outdma", last_even))
        if B4_NOC >= 2:
            wge(sp, "do1", ("outdma", last_odd))

    def B4_vector(v):
        for oc in range(B4_NOC):
            for pi in range(4):
                wge(v, f"ds{pi % 2}", ("w3st", oc, pi))
                if oc == 0 and pi == 0:
                    wge(v, "spe", ("x12", NHC - 1, 1, 1))
                if oc >= 2 and pi == 0:
                    wge(v, "spe", ("omm", oc - 2, 1))
                ins = E.x()
                if emit:
                    wsrc = io["wf32"][:, (pi % 2) * 1408:(pi % 2) * 1408 + 1408]
                    qdst = io["xnT"][:, (oc % 2) * 5632 + pi * 1408:
                                     (oc % 2) * 5632 + (pi + 1) * 1408]
                    ins = v.tensor_scalar(qdst, wsrc, io["tbc"][:, 6:7], -1.0,
                                          ALU.is_lt, ALU.mult)
                inc(ins, "sv", ("q31", oc, pi))
                wge(v, "sv", ("q31", oc, pi))
                ins = E.x()
                if emit:
                    wsrc = io["wf32"][:, (pi % 2) * 1408:(pi % 2) * 1408 + 1408]
                    qdst = io["xnT"][:, (oc % 2) * 5632 + pi * 1408:
                                     (oc % 2) * 5632 + (pi + 1) * 1408]
                    ins = v.scalar_tensor_tensor(qdst, wsrc, io["tbc"][:, 2:3],
                                                 qdst, ALU.is_gt, ALU.add)
                inc(ins, "sv", ("w3q", oc, pi))
            if oc >= 1:
                for th in range(2):
                    wge(v, "spe", ("otr", oc - 1, th))
                    wge(v, f"dx{(oc - 1) % 2}", ("xslab", oc - 1))
                    tb = th % 2
                    ins = E.x()
                    if emit:
                        xs = io["xsl"][:, ((oc - 1) % 2) * 1024 + th * 512:
                                       ((oc - 1) % 2) * 1024 + (th + 1) * 512]
                        ins = v.tensor_tensor(xs, io["ps_tb"][tb][:, 0:512],
                                              xs, ALU.add)
                    tb_done(ins, tb)
        oc = B4_NOC - 1
        for th in range(2):
            wge(v, "spe", ("otr", oc, th))
            wge(v, f"dx{oc % 2}", ("xslab", oc))
            tb = th % 2
            ins = E.x()
            if emit:
                xs = io["xsl"][:, (oc % 2) * 1024 + th * 512:
                               (oc % 2) * 1024 + (th + 1) * 512]
                ins = v.tensor_tensor(xs, io["ps_tb"][tb][:, 0:512],
                                      xs, ALU.add)
            tb_done(ins, tb)

    def B4_scalar(s):
        for oc in range(B4_NOC):
            for th in range(2):
                wge(s, "spe", ("omm", oc, th))
                if oc >= 1:
                    wge(s, "spe", ("otr", oc - 1, th))
                k = (oc * 2 + th) % 4
                ins = E.x()
                if emit:
                    ins = s.copy(io["ev2"][:, th * 512:(th + 1) * 512],
                                 io["ps_r"][k][:, :])
                rot_done(ins, k)

    def emit_tr(t, oc):
        for th in range(2):
            uwait(t, ("ot", oc, th))
            tb = th % 2
            tb_wait(t, tb, key=("otr", oc, th))
            ins = E.x()
            for blk_i in range(4):
                if emit:
                    src = io["ev2"][:, th * 512 + blk_i * 128:
                                    th * 512 + (blk_i + 1) * 128]
                    ins = t.transpose(
                        io["ps_tb"][tb][:, blk_i * 128:(blk_i + 1) * 128],
                        src, io["idb"][:, :])
            inc(ins, "spe", ("otr", oc, th))

    def B4_tensor(t):
        for oc in range(B4_NOC):
            wge(t, f"dw{oc % 2}", ("woslab", oc))
            ks = [(oc * 2) % 4, (oc * 2 + 1) % 4]
            for th in range(2):
                rot_wait(t, ks[th], key=("ot", oc, th))
            ins = E.x()
            for th in range(2):
                for cc in range(NCC):
                    if emit:
                        lhsT = io["wsb"][:, (oc % 2) * 2048 + cc * 128:
                                         (oc % 2) * 2048 + (cc + 1) * 128]
                        rhs = io["attn_pre"][:, cc * 1024 + th * 512:
                                             cc * 1024 + (th + 1) * 512]
                        ins = t.matmul(io["ps_r"][ks[th]][:, :], lhsT, rhs,
                                       start=(cc == 0), stop=False)
            inc(ins, "spe", ("wo_done", oc))
            if oc >= 1:
                emit_tr(t, oc - 1)
            wge(t, "sv", ("w3q", oc, 3))
            for th in range(2):
                ins = E.x()
                for hc in range(NHC):
                    if emit:
                        qoff = (oc % 2) * 5632 + hc * 128
                        lhsT = io["xnT"][:, qoff:qoff + 128]
                        rhs = io["arena"][:, hc * 1024 + th * 512:
                                          hc * 1024 + (th + 1) * 512]
                        ins = t.matmul(io["ps_r"][ks[th]][:, :], lhsT, rhs,
                                       start=False, stop=(hc == NHC - 1))
                inc(ins, "spe", ("omm", oc, th))
        emit_tr(t, B4_NOC - 1)

    nph = NPHASES
    if emit:
        if nph >= 4:
            with nc.Block("bmain") as blk:
                blk.sync(B0_sync)
                blk.tensor(B0_tensor)
                blk.tensor(B1_tensor)
                blk.tensor(B2_tensor)
                blk.tensor(B3_tensor)
                blk.tensor(B4_tensor)
                blk.gpsimd(B1_gp)
                blk.gpsimd(B2_gpA)
                blk.gpsimd(B1_gp2)
                blk.gpsimd(B0_gp_ccg)
                blk.gpsimd(B2_gp)
                blk.gpsimd(B4_gp)
                blk.sync(B0_gamma_sync)
                blk.sync(B3_sync)
                blk.sync(B4_sync)
                blk.scalar(B0_scalar)
                blk.scalar(B1_scalar)
                blk.scalar(B2_scalar)
                blk.scalar(B3_scalar)
                blk.scalar(B4_scalar)
                blk.vector(B0_vector)
                blk.vector(B1_vector)
                blk.vector(B2_vector)
                blk.vector(B3_vector)
                blk.vector(B4_vector)

    else:
        if nph >= 5:
            B0_sync(E); B0_tensor(E); B1_tensor(E); B2_tensor(E); B3_tensor(E)
            B4_tensor(E)
            B1_gp(E); B2_gpA(E); B1_gp2(E); B0_gp_ccg(E); B2_gp(E); B4_gp(E)
            B0_gamma_sync(E); B3_sync(E); B4_sync(E)
            B0_scalar(E); B1_scalar(E); B2_scalar(E); B3_scalar(E); B4_scalar(E)
            B0_vector(E); B1_vector(E); B2_vector(E); B3_vector(E); B4_vector(E)
    return mil


def make_nc():
    mil = build_program(emit=False)
    nc = bass.Bass("TRN2", target_bir_lowering=False, debug=False)
    build_program(emit=True, nc=nc, mil=mil)
    return nc


def kernel(L0_attn_x, L0_attn_cos, L0_attn_sin, L0_attn_Wq, L0_attn_Wk,
           L0_attn_Wv, L0_attn_Wo, L0_ffn_w1, L0_ffn_w2, L0_ffn_w3):
    assert np.asarray(L0_attn_cos).shape == (4096, 128)
    assert np.asarray(L0_attn_sin).shape == (4096, 128)
    x = np.ascontiguousarray(
        np.asarray(L0_attn_x, dtype=np.float32).reshape(8192, C))
    wqT = np.asarray(L0_attn_Wq, dtype=np.float32).T
    wkT = np.asarray(L0_attn_Wk, dtype=np.float32).T
    wvT = np.asarray(L0_attn_Wv, dtype=np.float32).T
    woT = np.asarray(L0_attn_Wo, dtype=np.float32).T
    w1T = np.asarray(L0_ffn_w1, dtype=np.float32).T
    w2T = np.asarray(L0_ffn_w2, dtype=np.float32).T
    w3T = np.asarray(L0_ffn_w3, dtype=np.float32).T
    idb = np.eye(128).astype(ml_dtypes.bfloat16)

    def slabify(wt, nslab, width):
        kk = wt.shape[0] // 128
        a = wt.reshape(kk, 128, nslab, width).transpose(2, 1, 0, 3)
        return np.ascontiguousarray(a.reshape(nslab, 128, kk * width))

    wqS = slabify(wqT, H, 128)
    wkS = slabify(wkT, 8, 256)
    wvS = slabify(wvT, 8, 256)
    woS = slabify(woT, NCC, 128)
    w1S = slabify(w1T, NHC, 128)
    w2S = slabify(w2T, NHC, 128)
    w3S = slabify(w3T, NCC, 128)

    nr12 = C // NCORE
    nr3 = HF // NCORE
    in_maps = []
    for c in range(NCORE):
        g1 = np.ascontiguousarray(
            w1T[c * nr12:(c + 1) * nr12, :]).reshape(128, GCOLS)
        g2 = np.ascontiguousarray(
            w2T[c * nr12:(c + 1) * nr12, :]).reshape(128, GCOLS)
        g3 = np.ascontiguousarray(
            w3T[c * nr3:(c + 1) * nr3, :]).reshape(128, GCOLS)
        xc = np.ascontiguousarray(x[c * TC:(c + 1) * TC, :])
        xcs = np.ascontiguousarray(
            xc.reshape(NT, 128, NCC, 128).transpose(2, 1, 0, 3)).reshape(
                NCC, 128, NT * 128)
        in_maps.append({
            "x": xc, "xcs": xcs,
            "wqS": wqS, "wkS": wkS, "wvS": wvS, "woS": woS,
            "w1S": w1S, "w2S": w2S, "w3S": w3S,
            "g1": g1, "g2": g2, "g3": g3,
            "idb": idb,
        })

    nc = make_nc()
    res = run_bass_kernel_spmd(nc, in_maps, core_ids=list(range(NCORE)))
    out = np.concatenate([np.asarray(r["out"]) for r in res.results], axis=0)
    return out.reshape(1, 8192, C).astype(np.float32)


# version 20
# speedup vs baseline: 1.0022x; 1.0022x over previous
import numpy as np
import ml_dtypes
import concourse.bass as bass
import concourse.mybir as mybir
from concourse.bass_utils import run_bass_kernel_spmd

F32 = mybir.dt.float32
BF16 = mybir.dt.bfloat16
AF = mybir.ActivationFunctionType
ALU = mybir.AluOpType
AX = mybir.AxisListType

NCORE = 8
TC = 1024
NT = 8
C = 2048
NCC = 16
H = 16
NHC = 44
HF = 5632
NW = HF * C
EPS_LN = 1e-5
EPS_Q = 1e-8
EPS_Z = 1e-6

GCOLS = NW // NCORE // 128
NGP = 8
GP = GCOLS // NGP

NPHASES = 5
USE_CC = 1
B4_NOC = 16
DEBUG_DUMP = 0

KBUF = 0
VST = 16384
KVBF = 20480


class Mil:
    def __init__(self):
        self.c = {}
        self.m = {}

    def bump(self, sem, key=None, k=1):
        self.c[sem] = self.c.get(sem, 0) + k
        if key is not None:
            kk = (sem, key)
            assert kk not in self.m, kk
            self.m[kk] = self.c[sem]
        return self.c[sem]

    def at(self, sem, key):
        return self.m[(sem, key)]


class _DI:
    def then_inc(self, *a, **k):
        return self


class _DummyEng:
    def __getattr__(self, name):
        return lambda *a, **k: _DI()


SEM_NAMES = [
    "dgp", "dmisc", "dx0", "dx1", "dg0", "dg1", "dw0", "dw1",
    "dq0", "dq1", "ds0", "ds1", "do0", "do1", "sv", "sa", "spe", "cc",
    "rb0", "rb1", "rb2", "rb3", "tb0", "tb1", "kb0", "kb1",
]


def build_program(emit, nc=None, mil=None):
    if mil is None:
        mil = Mil()

    class B:
        pass

    b = B()
    b.emit = emit
    E = _DummyEng()

    def inc(instr, sem, key=None, k=1):
        if emit:
            instr.then_inc(b.sems[sem], k)
        else:
            mil.bump(sem, key, k)
        return instr

    def wge(eng, sem, key):
        if emit:
            eng.wait_ge(b.sems[sem], mil.at(sem, key))

    def wval(eng, sem, val):
        if emit:
            eng.wait_ge(b.sems[sem], val)

    io = {}
    if emit:
        b.sems = {n: nc.alloc_semaphore(n) for n in SEM_NAMES}

        io["dx"] = nc.dram_tensor("x", [TC, C], F32, kind="ExternalInput").ap()
        io["dxcs"] = nc.dram_tensor("xcs", [NCC, 128, 1024], F32,
                                    kind="ExternalInput").ap()
        io["dwq"] = nc.dram_tensor("wqS", [H, 128, 2048], F32,
                                   kind="ExternalInput").ap()
        io["dwk"] = nc.dram_tensor("wkS", [8, 128, 4096], F32,
                                   kind="ExternalInput").ap()
        io["dwv"] = nc.dram_tensor("wvS", [8, 128, 4096], F32,
                                   kind="ExternalInput").ap()
        io["dwo"] = nc.dram_tensor("woS", [NCC, 128, 2048], F32,
                                   kind="ExternalInput").ap()
        io["dw1"] = nc.dram_tensor("w1S", [NHC, 128, 2048], F32,
                                   kind="ExternalInput").ap()
        io["dw2"] = nc.dram_tensor("w2S", [NHC, 128, 2048], F32,
                                   kind="ExternalInput").ap()
        io["dw3"] = nc.dram_tensor("w3S", [NCC, 128, 5632], F32,
                                   kind="ExternalInput").ap()
        io["dg1"] = nc.dram_tensor("g1", [128, GCOLS], F32, kind="ExternalInput").ap()
        io["dg2"] = nc.dram_tensor("g2", [128, GCOLS], F32, kind="ExternalInput").ap()
        io["dg3"] = nc.dram_tensor("g3", [128, GCOLS], F32, kind="ExternalInput").ap()
        io["didb"] = nc.dram_tensor("idb", [128, 128], BF16, kind="ExternalInput").ap()
        io["dout"] = nc.dram_tensor("out", [TC, C], F32, kind="ExternalOutput").ap()
        if DEBUG_DUMP:
            io["dbg_tbc"] = nc.dram_tensor("dbg_tbc", [128, 8], F32,
                                           kind="ExternalOutput").ap()
            io["dbg_ks"] = nc.dram_tensor("dbg_ks", [128, 16], BF16,
                                          kind="ExternalOutput").ap()
            io["dbg_ap"] = nc.dram_tensor("dbg_ap", [128, 2048], BF16,
                                          kind="ExternalOutput").ap()
            io["dbg_hid"] = nc.dram_tensor("dbg_hid", [128, 2048], BF16,
                                           kind="ExternalOutput").ap()
            io["dbg_xnt"] = nc.dram_tensor("dbg_xnt", [128, 2048], BF16,
                                           kind="ExternalOutput").ap()
            io["dbg_kv"] = nc.dram_tensor("dbg_kv", [128, 2048], BF16,
                                          kind="ExternalOutput").ap()

        io["ccg_in"] = nc.dram_tensor("ccg_in", [1, 4], F32)
        io["ccg_out"] = nc.dram_tensor("ccg_out", [1, 4], F32)
        io["cck_in"] = nc.dram_tensor("cck_in", [128, 2064], F32)
        io["cck_out"] = nc.dram_tensor("cck_out", [128, 2064], F32)

        A = nc.alloc_sbuf_tensor
        io["xnT"] = A("xnT", [128, 16384], BF16)
        io["arena"] = A("arena", [128, NHC * 1024], BF16)
        io["attn_pre"] = A("attn_pre", [128, 16384], BF16)
        io["wsb"] = A("wsb", [128, 9728], BF16)
        io["wf32"] = A("wf32", [128, 4096], F32)
        io["qh"] = A("qh", [128, 2048], BF16)
        io["ev1"] = A("ev1", [128, 1024], BF16)
        io["ev2"] = A("ev2", [128, 1024], BF16)
        io["xsl"] = A("xsl", [128, 2048], F32)
        io["ksum_sb"] = A("ksum_sb", [128, 16], F32)
        io["ksum_bf"] = A("ksum_bf", [128, 16], BF16)
        io["lnst"] = A("lnst", [128, 8], F32)
        io["gacc"] = A("gacc", [128, 24], F32)
        io["gmat"] = A("gmat", [128, 4], F32)
        io["tsb"] = A("tsb", [1, 4], F32)
        io["gsum"] = A("gsum", [1, 4], F32)
        io["tbc"] = A("tbc", [128, 8], F32)
        io["idb"] = A("idb_sb", [128, 128], BF16)
        io["ones1b"] = A("ones1b", [1, 128], BF16)
        io["ones1f"] = A("ones1f", [1, 128], F32)
        io["onecf"] = A("onecf", [128, 1], F32)
        io["onecb"] = A("onecb", [128, 1], BF16)
        io["epsln"] = A("epsln", [128, 1], F32)

        P = nc.alloc_psum_tensor
        io["ps_r"] = [P(f"ps_r{i}", [128, 512], F32) for i in range(4)]
        io["ps_tb"] = [P("ps_tb0", [128, 1024], BF16),
                       P("ps_tb1", [128, 1024], BF16)]
        io["ps_kv"] = [P("ps_kv0", [128, 512], F32), P("ps_kv1", [128, 512], F32)]

    rb_use = [0, 0, 0, 0]
    tb_use = [0, 0]
    kb_use = [0, 0]
    U = {}

    def rot_wait(eng, k, key=None):
        if rb_use[k] > 0:
            wval(eng, f"rb{k}", rb_use[k])
        rb_use[k] += 1
        if key is not None:
            U[key] = (f"rb{k}", rb_use[k])
        return rb_use[k]

    def rot_done(instr, k):
        inc(instr, f"rb{k}")

    def uwait(eng, key):
        sem, val = U[key]
        wval(eng, sem, val)

    def tb_wait(eng, k, key=None):
        if tb_use[k] > 0:
            wval(eng, f"tb{k}", tb_use[k])
        tb_use[k] += 1
        if key is not None:
            U[key] = (f"tb{k}", tb_use[k])
        return tb_use[k]

    def tb_done(instr, k):
        inc(instr, f"tb{k}")

    def kb_wait(eng, k, key=None):
        if kb_use[k] > 0:
            wval(eng, f"kb{k}", kb_use[k])
        kb_use[k] += 1
        if key is not None:
            U[key] = (f"kb{k}", kb_use[k])
        return kb_use[k]

    def kb_done(instr, k):
        inc(instr, f"kb{k}")

    def B0_sync(sp):
        inc(sp.dma_start(out=io["idb"][:, :], in_=io["didb"]) if emit else E.x(),
            "dmisc", ("idb",), 16)
        for tc in range(NT):
            if tc >= 2:
                wge(sp, "sv", ("cast", tc - 2))
            ins = E.x()
            if emit:
                src = io["dx"][tc * 128:(tc + 1) * 128, :]
                dst = io["wf32"][:, (tc % 2) * 2048:(tc % 2) * 2048 + 2048]
                ins = sp.dma_start(out=dst, in_=src)
            inc(ins, f"dx{tc % 2}", ("x", tc), 16)

    def B0_gamma_sync(sp):
        for m in range(3):
            for p in range(NGP):
                k = m * NGP + p
                if k < 2:
                    wge(sp, "sv", ("cast", NT - 1))
                else:
                    wge(sp, "sv", ("gred", k - 2))
                ins = E.x()
                if emit:
                    g = [io["dg1"], io["dg2"], io["dg3"]][m]
                    dst = io["wf32"][:, (k % 2) * 2048:(k % 2) * 2048 + GP]
                    ins = sp.dma_start(out=dst, in_=g[:, p * GP:(p + 1) * GP])
                inc(ins, f"dg{k % 2}", ("gpiece", k), 16)

    def B0_gp_ccg(gp):
        uwait(gp, ("gmm_ev",))
        inc(gp.dma_start(out=io["ccg_in"][:, :], in_=io["tsb"][:, :])
            if emit else E.x(), "dgp", ("ccg_in",), 16)
        wge(gp, "dgp", ("ccg_in",))
        ins = E.x()
        if emit and USE_CC:
            ins = gp.collective_compute(
                "AllReduce", ALU.add, replica_groups=[list(range(NCORE))],
                ins=[io["ccg_in"].ap()], outs=[io["ccg_out"].ap()])
        elif emit:
            ins = gp.nop()
        inc(ins, "cc", ("arg",), 1)

    def B0_scalar(s):
        for tc in range(NT):
            wge(s, f"dx{tc % 2}", ("x", tc))
            ins = E.x()
            if emit:
                sx = io["wf32"][:, (tc % 2) * 2048:(tc % 2) * 2048 + 2048]
                ins = s.activation(io["xsl"][:, :], sx, AF.Square,
                                   accum_out=io["lnst"][:, 2:3])
            inc(ins, "sa", ("sq", tc))
            wge(s, "sv", ("var", tc))
            ins = E.x()
            if emit:
                ins = s.activation(io["lnst"][:, 5:6], io["lnst"][:, 4:5],
                                   AF.Ln, bias=io["epsln"][:, 0:1], scale=1.0)
            inc(ins, "sa", ("lnv", tc))
            wge(s, "sa", ("lnv", tc))
            ins = E.x()
            if emit:
                ins = s.activation(io["lnst"][:, 6:7], io["lnst"][:, 5:6],
                                   AF.Exp, scale=-0.5)
            inc(ins, "sa", ("rstd", tc))

    def B0_vector(v):
        ins = E.x()
        if emit:
            v.memset(io["ones1b"][:, :], 1.0)
            v.memset(io["ones1f"][:, :], 1.0)
            v.memset(io["onecf"][:, :], 1.0)
            v.memset(io["onecb"][:, :], 1.0)
            v.memset(io["tsb"][:, :], 0.0)
            v.memset(io["gmat"][:, :], 0.0)
            ins = v.memset(io["epsln"][:, :], EPS_LN)
        inc(ins, "sv", ("memset",))
        for tc in range(NT):
            wge(v, f"dx{tc % 2}", ("x", tc))
            ins = E.x()
            if emit:
                sx = io["wf32"][:, (tc % 2) * 2048:(tc % 2) * 2048 + 2048]
                ins = v.tensor_reduce(io["lnst"][:, 0:1], sx, AX.X, ALU.add)
            inc(ins, "sv", ("r1", tc))
            wge(v, "sv", ("r1", tc))
            ins = E.x()
            if emit:
                ins = v.tensor_scalar(io["lnst"][:, 1:2], io["lnst"][:, 0:1],
                                      -1.0 / C, None, ALU.mult)
            inc(ins, "sv", ("nm", tc))
            wge(v, "sa", ("sq", tc))
            wge(v, "sv", ("nm", tc))
            ins = E.x()
            if emit:
                ins = v.tensor_tensor(io["lnst"][:, 3:4], io["lnst"][:, 1:2],
                                      io["lnst"][:, 1:2], ALU.mult)
            inc(ins, "sv", ("m2", tc))
            wge(v, "sv", ("m2", tc))
            ins = E.x()
            if emit:
                ins = v.scalar_tensor_tensor(io["lnst"][:, 4:5],
                                             io["lnst"][:, 2:3], 1.0 / C,
                                             io["lnst"][:, 3:4],
                                             ALU.mult, ALU.subtract)
            inc(ins, "sv", ("var", tc))
            wge(v, "sa", ("rstd", tc))
            ins = E.x()
            if emit:
                ins = v.tensor_tensor(io["lnst"][:, 7:8], io["lnst"][:, 1:2],
                                      io["lnst"][:, 6:7], ALU.mult)
            inc(ins, "sv", ("bb", tc))
            wge(v, "sv", ("bb", tc))
            ins = E.x()
            if emit:
                xn = io["attn_pre"][:, 0:2048]
                ins = v.tensor_scalar(xn, sx, io["lnst"][:, 6:7],
                                      io["lnst"][:, 7:8], ALU.mult, ALU.add)
            inc(ins, "sv", ("cast", tc))
            for r in range(2):
                wge(v, "spe", ("xtr", tc, r))
                ins = E.x()
                if emit:
                    dst = io["xnT"].ap().rearrange("p (cc t) -> p cc t", cc=NCC)
                    dst = dst[:, r * 8:(r + 1) * 8, tc * 128:(tc + 1) * 128]
                    src = io["ps_tb"][r].ap().rearrange("p (z t) -> p z t", z=8)
                    ins = v.tensor_copy(dst, src)
                tb_done(ins, r)
    def B0_tensor(t):
        wge(t, "dmisc", ("idb",))
        wge(t, "sv", ("memset",))
        for tc in range(NT):
            wge(t, "sv", ("cast", tc))
            for r in range(2):
                tb_wait(t, r, key=("xnt", tc, r))
                ins = E.x()
                for blk_i in range(8):
                    if emit:
                        cc = r * 8 + blk_i
                        src = io["attn_pre"][:, cc * 128:(cc + 1) * 128]
                        ins = t.transpose(
                            io["ps_tb"][r][:, blk_i * 128:(blk_i + 1) * 128],
                            src, io["idb"][:, :])
                inc(ins, "spe", ("xtr", tc, r))
            if tc < 4:
                if tc == 0:
                    wge(t, "dw0", ("kslab", 0))
                uwait(t, ("xnt", tc, 0))
                uwait(t, ("xnt", tc, 1))
                k = tc % 4
                rot_wait(t, k, key=("Kt", 0, tc))
                ins = E.x()
                for cc in range(NCC):
                    if emit:
                        lhsT = io["xnT"][:, cc * 1024 + tc * 128:
                                         cc * 1024 + (tc + 1) * 128]
                        rhs = wslab(0)[:, cc * 256:(cc + 1) * 256]
                        ins = t.matmul(io["ps_r"][k][:, 0:256], lhsT, rhs,
                                       start=(cc == 0), stop=(cc == NCC - 1))
                inc(ins, "spe", ("kproj", 0, tc))

    def wslab(i):
        return io["wsb"][:, (i % 2) * 4096:(i % 2) * 4096 + 4096]

    def B1_gp(gp):
        for part, dwn in (("k", "dwk"), ("v", "dwv")):
            for osl in range(8):
                if part == "k":
                    if osl >= 2:
                        wge(gp, "spe", ("kproj", osl - 2, NT - 1))
                else:
                    if osl <= 1:
                        wge(gp, "spe", ("kproj", 6 + osl, NT - 1))
                    else:
                        wge(gp, "spe", ("vproj", osl - 2, NT - 1))
                ins = E.x()
                if emit:
                    ins = gp.dma_start(out=wslab(osl), in_=io[dwn][osl])
                inc(ins, f"dw{osl % 2}", (part + "slab", osl), 16)

    def B1_tensor(t):
        for tc in range(4, NT):
            uwait(t, ("xnt", tc, 0))
            uwait(t, ("xnt", tc, 1))
            k = tc % 4
            rot_wait(t, k, key=("Kt", 0, tc))
            ins = E.x()
            for cc in range(NCC):
                if emit:
                    lhsT = io["xnT"][:, cc * 1024 + tc * 128:
                                     cc * 1024 + (tc + 1) * 128]
                    rhs = wslab(0)[:, cc * 256:(cc + 1) * 256]
                    ins = t.matmul(io["ps_r"][k][:, 0:256], lhsT, rhs,
                                   start=(cc == 0), stop=(cc == NCC - 1))
            inc(ins, "spe", ("kproj", 0, tc))
        tix = NT
        for osl in range(1, 8):
            wge(t, f"dw{osl % 2}", ("kslab", osl))
            for tc in range(NT):
                k = tix % 4
                rot_wait(t, k, key=("Kt", osl, tc))
                ins = E.x()
                for cc in range(NCC):
                    if emit:
                        lhsT = io["xnT"][:, cc * 1024 + tc * 128:
                                         cc * 1024 + (tc + 1) * 128]
                        rhs = wslab(osl)[:, cc * 256:(cc + 1) * 256]
                        ins = t.matmul(io["ps_r"][k][:, 0:256], lhsT, rhs,
                                       start=(cc == 0), stop=(cc == NCC - 1))
                inc(ins, "spe", ("kproj", osl, tc))
                tix += 1
        for osl in range(8):
            wge(t, f"dw{osl % 2}", ("vslab", osl))
            for tc in range(NT):
                k = tix % 4
                rot_wait(t, k, key=("Vt", osl, tc))
                ins = E.x()
                for cc in range(NCC):
                    if emit:
                        lhsT = io["xnT"][:, cc * 1024 + tc * 128:
                                         cc * 1024 + (tc + 1) * 128]
                        rhs = wslab(osl)[:, cc * 256:(cc + 1) * 256]
                        ins = t.matmul(io["ps_r"][k][:, 0:256], lhsT, rhs,
                                       start=(cc == 0), stop=(cc == NCC - 1))
                inc(ins, "spe", ("vproj", osl, tc))
                tix += 1
            if osl == 0:
                wge(t, "sv", ("gmat",))
                rot_wait(t, 0, key=("gmm_ev",))
                inc(t.matmul(io["ps_r"][0][0:1, 0:4], io["onecf"][:, 0:1],
                             io["gmat"][:, 0:4], start=True, stop=True)
                    if emit else E.x(), "spe", ("gmm",))
            uwait(t, ("Vt", osl, NT - 1))
            uwait(t, ("Kt", osl, NT - 1))
            g = osl % 2
            kb_wait(t, g, key=("kvg", osl))
            ins = E.x()
            first = True
            for hl in range(2):
                for tc in range(NT):
                    if emit:
                        base = KBUF + tc * 2048 + osl * 256 + hl * 128
                        kv_lhsT = io["arena"][:, base:base + 128]
                        vb = VST + g * 2048 + tc * 256 + hl * 128
                        vs = io["arena"][:, vb:vb + 128]
                        t.matmul(io["ps_kv"][g][:, hl * 128:(hl + 1) * 128],
                                 kv_lhsT, vs, start=first, stop=False)
                        ins = t.matmul(io["ps_kv"][g][:, 256 + hl:257 + hl],
                                       kv_lhsT, io["onecb"][:, 0:1],
                                       start=False,
                                       stop=(hl == 1 and tc == NT - 1))
                    first = False
            inc(ins, "spe", ("kvmm", osl))

    def B1_scalar(s):
        for osl in range(8):
            for tc in range(NT):
                i = osl * NT + tc
                wge(s, "spe", ("kproj", osl, tc))
                if i >= 2:
                    io2 = ((i - 2) // NT, (i - 2) % NT)
                    wge(s, "sv", ("kmn", io2[0], io2[1]))
                k = i % 4
                ins = E.x()
                if emit:
                    e = io["ev1"][:, (i % 2) * 256:(i % 2) * 256 + 256]
                    ins = s.activation(e, io["ps_r"][k][:, 0:256], AF.Exp)
                inc(ins, "sa", ("kexp", osl, tc))
        for osl in range(8):
            for tc in range(NT):
                wge(s, "spe", ("vproj", osl, tc))
                k = (64 + osl * NT + tc) % 4
                ins = E.x()
                if emit:
                    vb = VST + (osl % 2) * 2048 + tc * 256
                    ins = s.copy(io["arena"][:, vb:vb + 256],
                                 io["ps_r"][k][:, 0:256])
                rot_done(ins, k)

    def B1_vector(v):
        for osl in range(8):
            for tc in range(NT):
                i = osl * NT + tc
                wge(v, "sa", ("kexp", osl, tc))
                k = i % 4
                ins = E.x()
                if emit:
                    e = io["ev1"][:, (i % 2) * 256:(i % 2) * 256 + 256]
                    t1 = io["ev1"][:, 512 + (i % 2) * 256:512 + (i % 2) * 256 + 256]
                    ins = v.tensor_single_scalar(t1, e, 1.0, ALU.min)
                inc(ins, "sv", ("kmn", osl, tc))
                wge(v, "sv", ("kmn", osl, tc))
                ins = E.x()
                if emit:
                    base = KBUF + tc * 2048 + osl * 256
                    ins = v.scalar_tensor_tensor(io["arena"][:, base:base + 256],
                                                 io["ps_r"][k][:, 0:256], 0.0,
                                                 t1, ALU.max, ALU.add)
                rot_done(ins, k)
            for kk in range(3):
                k2 = osl * 3 + kk
                wge(v, f"dg{k2 % 2}", ("gpiece", k2))
                ins = E.x()
                if emit:
                    sg = io["wf32"][:, (k2 % 2) * 2048:(k2 % 2) * 2048 + GP]
                    ins = v.tensor_reduce(io["gacc"][:, k2:k2 + 1], sg, AX.X,
                                          ALU.add, apply_absolute_value=True)
                inc(ins, "sv", ("gred", k2))
        wge(v, "sv", ("gred", 3 * NGP - 1))
        ins = E.x()
        if emit:
            for m in range(3):
                ins = v.tensor_reduce(io["gmat"][:, m:m + 1],
                                      io["gacc"][:, m * NGP:(m + 1) * NGP],
                                      AX.X, ALU.add)
        inc(ins, "sv", ("gmat",))
        wge(v, "spe", ("gmm",))
        rot_done(v.tensor_copy(io["tsb"][0:1, 0:4], io["ps_r"][0][0:1, 0:4])
                 if emit else E.x(), 0)
        for osl in range(8):
            wge(v, "spe", ("kvmm", osl))
            g = osl % 2
            ins = E.x()
            if emit:
                v.tensor_copy(io["wf32"][:, osl * 256:(osl + 1) * 256],
                              io["ps_kv"][g][:, 0:256])
                ins = v.tensor_copy(io["ksum_sb"][:, osl * 2:osl * 2 + 2],
                                    io["ps_kv"][g][:, 256:258])
            kb_done(ins, g)
        uwait(v, ("kvg", 7))
        inc(v.tensor_copy(io["wf32"][:, 2048:2064], io["ksum_sb"][:, :])
            if emit else E.x(), "sv", ("ksfix",))

    def B1_gp2(gp):
        wge(gp, "sv", ("ksfix",))
        inc(gp.dma_start(out=io["cck_in"][:, :], in_=io["wf32"][:, 0:2064])
            if emit else E.x(), "dgp", ("cck_in",), 16)
        wge(gp, "dgp", ("cck_in",))
        ins = E.x()
        if emit and USE_CC:
            ins = gp.collective_compute(
                "AllReduce", ALU.add, replica_groups=[list(range(NCORE))],
                ins=[io["cck_in"].ap()], outs=[io["cck_out"].ap()])
        elif emit:
            ins = gp.nop()
        inc(ins, "cc", ("ark",), 1)

    def wqslab(h):
        return io["wsb"][:, (h % 4) * 2048:(h % 4) * 2048 + 2048]

    def qsl(h, th):
        return io["arena"][:, h * 1024 + th * 512:h * 1024 + (th + 1) * 512]

    WQSEM = ["dw0", "dw1", "dq0", "dq1"]

    def B2_gpA(gp):
        for h in range(4):
            wge(gp, "spe", ("vproj", 7, NT - 1))
            ins = E.x()
            if emit:
                ins = gp.dma_start(out=wqslab(h), in_=io["dwq"][h])
            inc(ins, WQSEM[h % 4], ("wqslab", h), 16)

    def B2_gp(gp):
        for h in range(4, 6):
            wge(gp, "spe", ("qproj", h - 4, 1))
            ins = E.x()
            if emit:
                ins = gp.dma_start(out=wqslab(h), in_=io["dwq"][h])
            inc(ins, WQSEM[h % 4], ("wqslab", h), 16)
        wge(gp, "cc", ("arg",))
        inc(gp.dma_start(out=io["gsum"][:, :], in_=io["ccg_out"][:, :])
            if emit else E.x(), "dgp", ("gsum",), 16)
        for h in range(6, H):
            wge(gp, "spe", ("qproj", h - 4, 1))
            ins = E.x()
            if emit:
                ins = gp.dma_start(out=wqslab(h), in_=io["dwq"][h])
            inc(ins, WQSEM[h % 4], ("wqslab", h), 16)
        wge(gp, "cc", ("ark",))
        ins1 = E.x()
        ins2 = E.x()
        if emit:
            ins1 = gp.dma_start(out=io["arena"][:, KVBF:KVBF + 2048],
                                in_=io["cck_out"][:, 0:2048])
            ins2 = gp.dma_start(out=io["ksum_bf"][:, :],
                                in_=io["cck_out"][:, 2048:2064])
        inc(ins1, "dgp", None, 16)
        inc(ins2, "dgp", ("kvbf",), 16)

    def B2_tensor(t):
        for h in range(H):
            wge(t, WQSEM[h % 4], ("wqslab", h))
            for th in range(2):
                k = (h * 2 + th) % 4
                rot_wait(t, k, key=("Qt", h, th))
                ins = E.x()
                for cc in range(NCC):
                    if emit:
                        lhsT = wqslab(h)[:, cc * 128:(cc + 1) * 128]
                        rhs = io["xnT"][:, cc * 1024 + th * 512:
                                        cc * 1024 + (th + 1) * 512]
                        ins = t.matmul(io["ps_r"][k][:, :], lhsT, rhs,
                                       start=(cc == 0), stop=(cc == NCC - 1))
                inc(ins, "spe", ("qproj", h, th))
            if h == 8:
                wge(t, "sv", ("tcalc",))
                rot_wait(t, 0, key=("tbc_ev",))
                inc(t.matmul(io["ps_r"][0][:, 0:4], io["ones1f"][0:1, :],
                             io["tsb"][0:1, 0:4], start=True, stop=True)
                    if emit else E.x(), "spe", ("tbc",))
        wge(t, "dgp", ("kvbf",))
        for h in range(H):
            for th in range(2):
                uwait(t, ("Qt", h, th))
                kb_wait(t, th, key=("dent", h, th))
                ins = E.x()
                if emit:
                    ins = t.matmul(io["ps_kv"][th][0:1, :],
                                   io["ksum_bf"][:, h:h + 1], qsl(h, th),
                                   start=True, stop=True)
                inc(ins, "spe", ("den", h, th))
            for th in range(2):
                ka = (h * 2 + th) % 4
                rot_wait(t, ka, key=("att", h, th))
                ins = E.x()
                if emit:
                    ins = t.matmul(io["ps_r"][ka][:, :],
                                   io["arena"][:, KVBF + h * 128:
                                               KVBF + (h + 1) * 128],
                                   qsl(h, th), start=True, stop=True)
                inc(ins, "spe", ("attn", h, th))
            for th in range(2):
                wge(t, "sa", ("z", h, th))
                kz = (h * 2 + th + 2) % 4
                rot_wait(t, kz, key=("zbt", h, th))
                ins = E.x()
                if emit:
                    ins = t.matmul(io["ps_r"][kz][:, :], io["ones1f"][0:1, :],
                                   io["xsl"][0:1, th * 512:(th + 1) * 512],
                                   start=True, stop=True)
                inc(ins, "spe", ("zb", h, th))

    def B2_scalar(s):
        for h in range(H):
            for th in range(2):
                wge(s, "spe", ("qproj", h, th))
                if h >= 1:
                    wge(s, "sv", ("qmn", h - 1, th))
                k = (h * 2 + th) % 4
                ins = E.x()
                if emit:
                    e = io["ev1"][:, th * 512:(th + 1) * 512]
                    ins = s.activation(e, io["ps_r"][k][:, :], AF.Exp)
                inc(ins, "sa", ("qexp", h, th))
        for h in range(H):
            for th in range(2):
                uwait(s, ("dent", h, th))
                ins = E.x()
                if emit:
                    zs = io["xsl"][0:1, th * 512:(th + 1) * 512]
                    ins = s.activation(zs, zs, AF.Ln)
                inc(ins, "sa", ("lnz", h, th))
                wge(s, "sa", ("lnz", h, th))
                ins = E.x()
                if emit:
                    zs = io["xsl"][0:1, th * 512:(th + 1) * 512]
                    ins = s.activation(zs, zs, AF.Exp, scale=-1.0)
                inc(ins, "sa", ("z", h, th))
            for th in range(2):
                wge(s, "spe", ("attn", h, th))
                if h >= 1:
                    uwait(s, ("zbt", h - 1, th))
                ka = (h * 2 + th) % 4
                ins = E.x()
                if emit:
                    ins = s.copy(io["ev2"][:, th * 512:(th + 1) * 512],
                                 io["ps_r"][ka][:, :])
                rot_done(ins, ka)

    def B2_vector(v):
        for h in range(H):
            if h == 6:
                wge(v, "dgp", ("gsum",))
                inc(v.tensor_scalar(io["tsb"][0:1, 0:4], io["gsum"][0:1, 0:4],
                                    0.5 / NW, 0.5 * EPS_Q, ALU.mult, ALU.add)
                    if emit else E.x(), "sv", ("tcalc",))
            if h == 9:
                wge(v, "spe", ("tbc",))
                if emit:
                    inc(v.tensor_copy(io["tbc"][:, 0:4],
                                      io["ps_r"][0][:, 0:4]), "sv", ("tbcast",))
                    rot_done(v.tensor_scalar(io["tbc"][:, 4:8],
                                             io["ps_r"][0][:, 0:4],
                                             -1.0, None, ALU.mult), 0)
                else:
                    inc(E.x(), "sv", ("tbcast",))
                    rot_done(E.x(), 0)
            for th in range(2):
                wge(v, "sa", ("qexp", h, th))
                k = (h * 2 + th) % 4
                ins = E.x()
                if emit:
                    e = io["ev1"][:, th * 512:(th + 1) * 512]
                    t1 = io["ev2"][:, th * 512:(th + 1) * 512]
                    ins = v.tensor_single_scalar(t1, e, 1.0, ALU.min)
                inc(ins, "sv", ("qmn", h, th))
                wge(v, "sv", ("qmn", h, th))
                ins = E.x()
                if emit:
                    t1 = io["ev2"][:, th * 512:(th + 1) * 512]
                    ins = v.scalar_tensor_tensor(qsl(h, th), io["ps_r"][k][:, :],
                                                 0.0, t1, ALU.max, ALU.add)
                rot_done(ins, k)
        for h in range(H):
            for th in range(2):
                wge(v, "spe", ("den", h, th))
                ins = E.x()
                if emit:
                    zs = io["xsl"][0:1, th * 512:(th + 1) * 512]
                    ins = v.tensor_single_scalar(zs, io["ps_kv"][th][0:1, :],
                                                 EPS_Z, ALU.add)
                kb_done(ins, th)
            for th in range(2):
                uwait(v, ("att", h, th))
                wge(v, "spe", ("zb", h, th))
                kz = (h * 2 + th + 2) % 4
                ins = E.x()
                if emit:
                    dst = io["attn_pre"][:, h * 1024 + th * 512:
                                         h * 1024 + (th + 1) * 512]
                    ins = v.tensor_tensor(dst,
                                          io["ev2"][:, th * 512:(th + 1) * 512],
                                          io["ps_r"][kz][:, :], ALU.mult)
                rot_done(ins, kz)

    def B3_sync(sp):
        wge(sp, "dgp", ("cck_in",))
        for j in range(NHC):
            for m in range(2):
                if j >= 1:
                    wge(sp, "sv", ("quant", j - 1, m))
                ins = E.x()
                if emit:
                    dw = io["dw1"] if m == 0 else io["dw2"]
                    dst = io["wf32"][:, m * 2048:(m + 1) * 2048]
                    ins = sp.dma_start(out=dst, in_=dw[j])
                inc(ins, f"ds{m}", ("wst", j, m), 16)

    def B3_vector(v):
        for j in range(NHC):
            for m in range(2):
                wge(v, f"ds{m}", ("wst", j, m))
                if j <= 1:
                    wge(v, "spe", ("qproj", H - 1, 1))
                if j >= 2:
                    wge(v, "spe", ("x12", j - 2, m, 1))
                ins = E.x()
                if emit:
                    wsrc = io["wf32"][:, m * 2048:(m + 1) * 2048]
                    qoff = (m * 2 + j % 2) * 2048
                    qdst = io["wsb"][:, qoff:qoff + 2048]
                    ins = v.tensor_scalar(qdst, wsrc, io["tbc"][:, 4 + m:5 + m],
                                          -1.0, ALU.is_lt, ALU.mult)
                inc(ins, "sv", ("q1", j, m))
                wge(v, "sv", ("q1", j, m))
                ins = E.x()
                if emit:
                    ins = v.scalar_tensor_tensor(qdst, wsrc,
                                                 io["tbc"][:, m:m + 1],
                                                 qdst, ALU.is_gt, ALU.add)
                inc(ins, "sv", ("quant", j, m))
            for th in range(2):
                uwait(v, ("x12t", j, 0, th))
                wge(v, "spe", ("x12", j, 1, th))
                k2 = (j * 4 + 2 + th) % 4
                ins = E.x()
                if emit:
                    s1 = io["ev1"][:, th * 512:(th + 1) * 512]
                    dst = io["arena"][:, j * 1024 + th * 512:
                                      j * 1024 + (th + 1) * 512]
                    ins = v.tensor_tensor(dst, s1, io["ps_r"][k2][:, :],
                                          ALU.mult)
                rot_done(ins, k2)

    def B3_scalar(s):
        for j in range(NHC):
            for th in range(2):
                wge(s, "spe", ("x12", j, 0, th))
                if j >= 1:
                    uwait(s, ("x12t", j - 1, 1, th))
                k1 = (j * 4 + th) % 4
                ins = E.x()
                if emit:
                    ins = s.activation(io["ev1"][:, th * 512:(th + 1) * 512],
                                       io["ps_r"][k1][:, :], AF.Sigmoid)
                rot_done(ins, k1)

    def B3_tensor(t):
        for j in range(NHC):
            for m in range(2):
                wge(t, "sv", ("quant", j, m))
                for th in range(2):
                    k = (j * 4 + m * 2 + th) % 4
                    rot_wait(t, k, key=("x12t", j, m, th))
                    ins = E.x()
                    for cc in range(NCC):
                        if emit:
                            qoff = (m * 2 + j % 2) * 2048
                            lhsT = io["wsb"][:, qoff + cc * 128:
                                             qoff + (cc + 1) * 128]
                            rhs = io["xnT"][:, cc * 1024 + th * 512:
                                            cc * 1024 + (th + 1) * 512]
                            ins = t.matmul(io["ps_r"][k][:, :], lhsT, rhs,
                                           start=(cc == 0),
                                           stop=(cc == NCC - 1))
                    inc(ins, "spe", ("x12", j, m, th))

    def B4_gp(gp):
        wge(gp, "spe", ("x12", NHC - 1, 1, 1))
        for oc in range(B4_NOC):
            if oc >= 2:
                wge(gp, "spe", ("wo_done", oc - 2))
            ins = E.x()
            if emit:
                dst = io["wsb"][:, (oc % 2) * 2048:(oc % 2 + 1) * 2048]
                ins = gp.dma_start(out=dst, in_=io["dwo"][oc])
            inc(ins, f"dw{oc % 2}", ("woslab", oc), 16)

    def B4_sync(sp):
        wge(sp, "sv", ("quant", NHC - 1, 1))
        wge(sp, "spe", ("zb", H - 1, 1))
        for oc in range(B4_NOC):
            if oc >= 2:
                wge(sp, f"do{oc % 2}", ("outdma", oc - 2))
            ins = E.x()
            if emit:
                dst = io["xsl"][:, (oc % 2) * 1024:(oc % 2 + 1) * 1024]
                ins = sp.dma_start(out=dst, in_=io["dxcs"][oc])
            inc(ins, f"dx{oc % 2}", ("xslab", oc), 16)
            for pi in range(4):
                gpi = oc * 4 + pi
                if gpi >= 2:
                    poc, ppi = divmod(gpi - 2, 4)
                    wge(sp, "sv", ("w3q", poc, ppi))
                ins = E.x()
                if emit:
                    dst = io["wf32"][:, (pi % 2) * 1408:(pi % 2) * 1408 + 1408]
                    ins = sp.dma_start(out=dst,
                                       in_=io["dw3"][oc][:, pi * 1408:(pi + 1) * 1408])
                inc(ins, f"ds{pi % 2}", ("w3st", oc, pi), 16)
            if oc >= 1:
                uwait(sp, ("otr", oc - 1, 0))
                uwait(sp, ("otr", oc - 1, 1))
                ins = E.x()
                if emit:
                    dst = io["dout"].rearrange("(tc p) c -> p tc c", p=128)
                    dst = dst[:, :, (oc - 1) * 128:oc * 128]
                    src = io["xsl"][:, ((oc - 1) % 2) * 1024:((oc - 1) % 2 + 1) * 1024]
                    src = src.rearrange("p (tc c) -> p tc c", tc=NT)
                    ins = sp.dma_start(out=dst, in_=src)
                inc(ins, f"do{(oc - 1) % 2}", ("outdma", oc - 1), 16)
        oc = B4_NOC - 1
        uwait(sp, ("otr", oc, 0))
        uwait(sp, ("otr", oc, 1))
        ins = E.x()
        if emit:
            dst = io["dout"].rearrange("(tc p) c -> p tc c", p=128)
            dst = dst[:, :, oc * 128:(oc + 1) * 128]
            src = io["xsl"][:, (oc % 2) * 1024:(oc % 2 + 1) * 1024]
            src = src.rearrange("p (tc c) -> p tc c", tc=NT)
            ins = sp.dma_start(out=dst, in_=src)
        inc(ins, f"do{oc % 2}", ("outdma", oc), 16)
        last_even = (B4_NOC - 1) // 2 * 2
        last_odd = (B4_NOC - 2) // 2 * 2 + 1
        wge(sp, "do0", ("outdma", last_even))
        if B4_NOC >= 2:
            wge(sp, "do1", ("outdma", last_odd))

    def B4_vector(v):
        for oc in range(B4_NOC):
            for pi in range(4):
                wge(v, f"ds{pi % 2}", ("w3st", oc, pi))
                if oc == 0 and pi == 0:
                    wge(v, "spe", ("x12", NHC - 1, 1, 1))
                if oc >= 2 and pi == 0:
                    wge(v, "spe", ("omm", oc - 2, 1))
                ins = E.x()
                if emit:
                    wsrc = io["wf32"][:, (pi % 2) * 1408:(pi % 2) * 1408 + 1408]
                    qdst = io["xnT"][:, (oc % 2) * 5632 + pi * 1408:
                                     (oc % 2) * 5632 + (pi + 1) * 1408]
                    ins = v.tensor_scalar(qdst, wsrc, io["tbc"][:, 6:7], -1.0,
                                          ALU.is_lt, ALU.mult)
                inc(ins, "sv", ("q31", oc, pi))
                wge(v, "sv", ("q31", oc, pi))
                ins = E.x()
                if emit:
                    wsrc = io["wf32"][:, (pi % 2) * 1408:(pi % 2) * 1408 + 1408]
                    qdst = io["xnT"][:, (oc % 2) * 5632 + pi * 1408:
                                     (oc % 2) * 5632 + (pi + 1) * 1408]
                    ins = v.scalar_tensor_tensor(qdst, wsrc, io["tbc"][:, 2:3],
                                                 qdst, ALU.is_gt, ALU.add)
                inc(ins, "sv", ("w3q", oc, pi))
            if oc >= 1:
                for th in range(2):
                    wge(v, "spe", ("otr", oc - 1, th))
                    wge(v, f"dx{(oc - 1) % 2}", ("xslab", oc - 1))
                    tb = th % 2
                    ins = E.x()
                    if emit:
                        xs = io["xsl"][:, ((oc - 1) % 2) * 1024 + th * 512:
                                       ((oc - 1) % 2) * 1024 + (th + 1) * 512]
                        ins = v.tensor_tensor(xs, io["ps_tb"][tb][:, 0:512],
                                              xs, ALU.add)
                    tb_done(ins, tb)
        oc = B4_NOC - 1
        for th in range(2):
            wge(v, "spe", ("otr", oc, th))
            wge(v, f"dx{oc % 2}", ("xslab", oc))
            tb = th % 2
            ins = E.x()
            if emit:
                xs = io["xsl"][:, (oc % 2) * 1024 + th * 512:
                               (oc % 2) * 1024 + (th + 1) * 512]
                ins = v.tensor_tensor(xs, io["ps_tb"][tb][:, 0:512],
                                      xs, ALU.add)
            tb_done(ins, tb)

    def B4_scalar(s):
        for oc in range(B4_NOC):
            for th in range(2):
                wge(s, "spe", ("omm", oc, th))
                if oc >= 1:
                    wge(s, "spe", ("otr", oc - 1, th))
                k = (oc * 2 + th) % 4
                ins = E.x()
                if emit:
                    ins = s.copy(io["ev2"][:, th * 512:(th + 1) * 512],
                                 io["ps_r"][k][:, :])
                rot_done(ins, k)

    def emit_tr(t, oc):
        for th in range(2):
            uwait(t, ("ot", oc, th))
            tb = th % 2
            tb_wait(t, tb, key=("otr", oc, th))
            ins = E.x()
            for blk_i in range(4):
                if emit:
                    src = io["ev2"][:, th * 512 + blk_i * 128:
                                    th * 512 + (blk_i + 1) * 128]
                    ins = t.transpose(
                        io["ps_tb"][tb][:, blk_i * 128:(blk_i + 1) * 128],
                        src, io["idb"][:, :])
            inc(ins, "spe", ("otr", oc, th))

    def B4_tensor(t):
        for oc in range(B4_NOC):
            wge(t, f"dw{oc % 2}", ("woslab", oc))
            ks = [(oc * 2) % 4, (oc * 2 + 1) % 4]
            for th in range(2):
                rot_wait(t, ks[th], key=("ot", oc, th))
            ins = E.x()
            for th in range(2):
                for cc in range(NCC):
                    if emit:
                        lhsT = io["wsb"][:, (oc % 2) * 2048 + cc * 128:
                                         (oc % 2) * 2048 + (cc + 1) * 128]
                        rhs = io["attn_pre"][:, cc * 1024 + th * 512:
                                             cc * 1024 + (th + 1) * 512]
                        ins = t.matmul(io["ps_r"][ks[th]][:, :], lhsT, rhs,
                                       start=(cc == 0), stop=False)
            inc(ins, "spe", ("wo_done", oc))
            if oc >= 1:
                emit_tr(t, oc - 1)
            wge(t, "sv", ("w3q", oc, 3))
            for th in range(2):
                ins = E.x()
                for hc in range(NHC):
                    if emit:
                        qoff = (oc % 2) * 5632 + hc * 128
                        lhsT = io["xnT"][:, qoff:qoff + 128]
                        rhs = io["arena"][:, hc * 1024 + th * 512:
                                          hc * 1024 + (th + 1) * 512]
                        ins = t.matmul(io["ps_r"][ks[th]][:, :], lhsT, rhs,
                                       start=False, stop=(hc == NHC - 1))
                inc(ins, "spe", ("omm", oc, th))
        emit_tr(t, B4_NOC - 1)

    nph = NPHASES
    if emit:
        if nph >= 4:
            with nc.Block("bmain") as blk:
                blk.sync(B0_sync)
                blk.tensor(B0_tensor)
                blk.tensor(B1_tensor)
                blk.tensor(B2_tensor)
                blk.tensor(B3_tensor)
                blk.tensor(B4_tensor)
                blk.gpsimd(B1_gp)
                blk.gpsimd(B2_gpA)
                blk.gpsimd(B1_gp2)
                blk.gpsimd(B0_gp_ccg)
                blk.gpsimd(B2_gp)
                blk.gpsimd(B4_gp)
                blk.sync(B0_gamma_sync)
                blk.sync(B3_sync)
                blk.sync(B4_sync)
                blk.scalar(B0_scalar)
                blk.scalar(B1_scalar)
                blk.scalar(B2_scalar)
                blk.scalar(B3_scalar)
                blk.scalar(B4_scalar)
                blk.vector(B0_vector)
                blk.vector(B1_vector)
                blk.vector(B2_vector)
                blk.vector(B3_vector)
                blk.vector(B4_vector)

    else:
        if nph >= 5:
            B0_sync(E); B0_tensor(E); B1_tensor(E); B2_tensor(E); B3_tensor(E)
            B4_tensor(E)
            B1_gp(E); B2_gpA(E); B1_gp2(E); B0_gp_ccg(E); B2_gp(E); B4_gp(E)
            B0_gamma_sync(E); B3_sync(E); B4_sync(E)
            B0_scalar(E); B1_scalar(E); B2_scalar(E); B3_scalar(E); B4_scalar(E)
            B0_vector(E); B1_vector(E); B2_vector(E); B3_vector(E); B4_vector(E)
    return mil


def make_nc():
    mil = build_program(emit=False)
    nc = bass.Bass("TRN2", target_bir_lowering=False, debug=False)
    build_program(emit=True, nc=nc, mil=mil)
    return nc


def kernel(L0_attn_x, L0_attn_cos, L0_attn_sin, L0_attn_Wq, L0_attn_Wk,
           L0_attn_Wv, L0_attn_Wo, L0_ffn_w1, L0_ffn_w2, L0_ffn_w3):
    assert np.asarray(L0_attn_cos).shape == (4096, 128)
    assert np.asarray(L0_attn_sin).shape == (4096, 128)
    x = np.ascontiguousarray(
        np.asarray(L0_attn_x, dtype=np.float32).reshape(8192, C))
    wqT = np.asarray(L0_attn_Wq, dtype=np.float32).T
    wkT = np.asarray(L0_attn_Wk, dtype=np.float32).T
    wvT = np.asarray(L0_attn_Wv, dtype=np.float32).T
    woT = np.asarray(L0_attn_Wo, dtype=np.float32).T
    w1T = np.asarray(L0_ffn_w1, dtype=np.float32).T
    w2T = np.asarray(L0_ffn_w2, dtype=np.float32).T
    w3T = np.asarray(L0_ffn_w3, dtype=np.float32).T
    idb = np.eye(128).astype(ml_dtypes.bfloat16)

    def slabify(wt, nslab, width):
        kk = wt.shape[0] // 128
        a = wt.reshape(kk, 128, nslab, width).transpose(2, 1, 0, 3)
        return np.ascontiguousarray(a.reshape(nslab, 128, kk * width))

    wqS = slabify(wqT, H, 128)
    wkS = slabify(wkT, 8, 256)
    wvS = slabify(wvT, 8, 256)
    woS = slabify(woT, NCC, 128)
    w1S = slabify(w1T, NHC, 128)
    w2S = slabify(w2T, NHC, 128)
    w3S = slabify(w3T, NCC, 128)

    nr12 = C // NCORE
    nr3 = HF // NCORE
    in_maps = []
    for c in range(NCORE):
        g1 = np.ascontiguousarray(
            w1T[c * nr12:(c + 1) * nr12, :]).reshape(128, GCOLS)
        g2 = np.ascontiguousarray(
            w2T[c * nr12:(c + 1) * nr12, :]).reshape(128, GCOLS)
        g3 = np.ascontiguousarray(
            w3T[c * nr3:(c + 1) * nr3, :]).reshape(128, GCOLS)
        xc = np.ascontiguousarray(x[c * TC:(c + 1) * TC, :])
        xcs = np.ascontiguousarray(
            xc.reshape(NT, 128, NCC, 128).transpose(2, 1, 0, 3)).reshape(
                NCC, 128, NT * 128)
        in_maps.append({
            "x": xc, "xcs": xcs,
            "wqS": wqS, "wkS": wkS, "wvS": wvS, "woS": woS,
            "w1S": w1S, "w2S": w2S, "w3S": w3S,
            "g1": g1, "g2": g2, "g3": g3,
            "idb": idb,
        })

    nc = make_nc()
    res = run_bass_kernel_spmd(nc, in_maps, core_ids=list(range(NCORE)))
    out = np.concatenate([np.asarray(r["out"]) for r in res.results], axis=0)
    return out.reshape(1, 8192, C).astype(np.float32)


# version 21
# speedup vs baseline: 1.0189x; 1.0167x over previous
import numpy as np
import ml_dtypes
import concourse.bass as bass
import concourse.mybir as mybir
from concourse.bass_utils import run_bass_kernel_spmd

F32 = mybir.dt.float32
BF16 = mybir.dt.bfloat16
AF = mybir.ActivationFunctionType
ALU = mybir.AluOpType
AX = mybir.AxisListType

NCORE = 8
TC = 1024
NT = 8
C = 2048
NCC = 16
H = 16
NHC = 44
HF = 5632
NW = HF * C
EPS_LN = 1e-5
EPS_Q = 1e-8
EPS_Z = 1e-6

GCOLS = NW // NCORE // 128
NGP = 8
GP = GCOLS // NGP

NPHASES = 5
USE_CC = 1
B4_NOC = 16
DEBUG_DUMP = 0

KBUF = 0
VST = 16384
KVBF = 20480


class Mil:
    def __init__(self):
        self.c = {}
        self.m = {}

    def bump(self, sem, key=None, k=1):
        self.c[sem] = self.c.get(sem, 0) + k
        if key is not None:
            kk = (sem, key)
            assert kk not in self.m, kk
            self.m[kk] = self.c[sem]
        return self.c[sem]

    def at(self, sem, key):
        return self.m[(sem, key)]


class _DI:
    def then_inc(self, *a, **k):
        return self


class _DummyEng:
    def __getattr__(self, name):
        return lambda *a, **k: _DI()


SEM_NAMES = [
    "dgp", "dmisc", "dx0", "dx1", "dg0", "dg1", "dw0", "dw1",
    "dq0", "dq1", "ds0", "ds1", "do0", "do1", "sv", "sa", "spe", "cc",
    "rb0", "rb1", "rb2", "rb3", "tb0", "tb1", "kb0", "kb1",
]


def build_program(emit, nc=None, mil=None):
    if mil is None:
        mil = Mil()

    class B:
        pass

    b = B()
    b.emit = emit
    E = _DummyEng()

    def inc(instr, sem, key=None, k=1):
        if emit:
            instr.then_inc(b.sems[sem], k)
        else:
            mil.bump(sem, key, k)
        return instr

    def wge(eng, sem, key):
        if emit:
            eng.wait_ge(b.sems[sem], mil.at(sem, key))

    def wval(eng, sem, val):
        if emit:
            eng.wait_ge(b.sems[sem], val)

    io = {}
    if emit:
        b.sems = {n: nc.alloc_semaphore(n) for n in SEM_NAMES}

        io["dx"] = nc.dram_tensor("x", [TC, C], F32, kind="ExternalInput").ap()
        io["dxcs"] = nc.dram_tensor("xcs", [NCC, 128, 1024], F32,
                                    kind="ExternalInput").ap()
        io["dwq"] = nc.dram_tensor("wqS", [H, 128, 2048], F32,
                                   kind="ExternalInput").ap()
        io["dwk"] = nc.dram_tensor("wkS", [8, 128, 4096], F32,
                                   kind="ExternalInput").ap()
        io["dwv"] = nc.dram_tensor("wvS", [8, 128, 4096], F32,
                                   kind="ExternalInput").ap()
        io["dwo"] = nc.dram_tensor("woS", [NCC, 128, 2048], F32,
                                   kind="ExternalInput").ap()
        io["dw1"] = nc.dram_tensor("w1S", [NHC, 128, 2048], F32,
                                   kind="ExternalInput").ap()
        io["dw2"] = nc.dram_tensor("w2S", [NHC, 128, 2048], F32,
                                   kind="ExternalInput").ap()
        io["dw3"] = nc.dram_tensor("w3S", [NCC, 128, 5632], F32,
                                   kind="ExternalInput").ap()
        io["dg1"] = nc.dram_tensor("g1", [128, GCOLS], F32, kind="ExternalInput").ap()
        io["dg2"] = nc.dram_tensor("g2", [128, GCOLS], F32, kind="ExternalInput").ap()
        io["dg3"] = nc.dram_tensor("g3", [128, GCOLS], F32, kind="ExternalInput").ap()
        io["didb"] = nc.dram_tensor("idb", [128, 128], BF16, kind="ExternalInput").ap()
        io["dout"] = nc.dram_tensor("out", [TC, C], F32, kind="ExternalOutput").ap()
        if DEBUG_DUMP:
            io["dbg_tbc"] = nc.dram_tensor("dbg_tbc", [128, 8], F32,
                                           kind="ExternalOutput").ap()
            io["dbg_ks"] = nc.dram_tensor("dbg_ks", [128, 16], BF16,
                                          kind="ExternalOutput").ap()
            io["dbg_ap"] = nc.dram_tensor("dbg_ap", [128, 2048], BF16,
                                          kind="ExternalOutput").ap()
            io["dbg_hid"] = nc.dram_tensor("dbg_hid", [128, 2048], BF16,
                                           kind="ExternalOutput").ap()
            io["dbg_xnt"] = nc.dram_tensor("dbg_xnt", [128, 2048], BF16,
                                           kind="ExternalOutput").ap()
            io["dbg_kv"] = nc.dram_tensor("dbg_kv", [128, 2048], BF16,
                                          kind="ExternalOutput").ap()

        io["ccg_in"] = nc.dram_tensor("ccg_in", [1, 4], F32)
        io["ccg_out"] = nc.dram_tensor("ccg_out", [1, 4], F32)
        io["cck_in"] = nc.dram_tensor("cck_in", [128, 2064], F32)
        io["cck_out"] = nc.dram_tensor("cck_out", [128, 2064], F32)

        A = nc.alloc_sbuf_tensor
        io["xnT"] = A("xnT", [128, 16384], BF16)
        io["arena"] = A("arena", [128, NHC * 1024], BF16)
        io["attn_pre"] = A("attn_pre", [128, 16384], BF16)
        io["wsb"] = A("wsb", [128, 9728], BF16)
        io["wf32"] = A("wf32", [128, 4096], F32)
        io["qh"] = A("qh", [128, 2048], BF16)
        io["ev1"] = A("ev1", [128, 1024], BF16)
        io["ev2"] = A("ev2", [128, 1024], BF16)
        io["xsl"] = A("xsl", [128, 2048], F32)
        io["ksum_sb"] = A("ksum_sb", [128, 16], F32)
        io["ksum_bf"] = A("ksum_bf", [128, 16], BF16)
        io["lnst"] = A("lnst", [128, 8], F32)
        io["gacc"] = A("gacc", [128, 24], F32)
        io["gmat"] = A("gmat", [128, 4], F32)
        io["tsb"] = A("tsb", [1, 4], F32)
        io["gsum"] = A("gsum", [1, 4], F32)
        io["tbc"] = A("tbc", [128, 8], F32)
        io["idb"] = A("idb_sb", [128, 128], BF16)
        io["ones1b"] = A("ones1b", [1, 128], BF16)
        io["ones1f"] = A("ones1f", [1, 128], F32)
        io["onecf"] = A("onecf", [128, 1], F32)
        io["onecb"] = A("onecb", [128, 1], BF16)
        io["epsln"] = A("epsln", [128, 1], F32)

        P = nc.alloc_psum_tensor
        io["ps_r"] = [P(f"ps_r{i}", [128, 512], F32) for i in range(4)]
        io["ps_tb"] = [P("ps_tb0", [128, 1024], BF16),
                       P("ps_tb1", [128, 1024], BF16)]
        io["ps_kv"] = [P("ps_kv0", [128, 512], F32), P("ps_kv1", [128, 512], F32)]

    rb_use = [0, 0, 0, 0]
    tb_use = [0, 0]
    kb_use = [0, 0]
    U = {}

    def rot_wait(eng, k, key=None):
        if rb_use[k] > 0:
            wval(eng, f"rb{k}", rb_use[k])
        rb_use[k] += 1
        if key is not None:
            U[key] = (f"rb{k}", rb_use[k])
        return rb_use[k]

    def rot_done(instr, k):
        inc(instr, f"rb{k}")

    def uwait(eng, key):
        sem, val = U[key]
        wval(eng, sem, val)

    def tb_wait(eng, k, key=None):
        if tb_use[k] > 0:
            wval(eng, f"tb{k}", tb_use[k])
        tb_use[k] += 1
        if key is not None:
            U[key] = (f"tb{k}", tb_use[k])
        return tb_use[k]

    def tb_done(instr, k):
        inc(instr, f"tb{k}")

    def kb_wait(eng, k, key=None):
        if kb_use[k] > 0:
            wval(eng, f"kb{k}", kb_use[k])
        kb_use[k] += 1
        if key is not None:
            U[key] = (f"kb{k}", kb_use[k])
        return kb_use[k]

    def kb_done(instr, k):
        inc(instr, f"kb{k}")

    XSEM = ["dx0", "dx1", "dg0"]

    def xstg(tc):
        return (io["wf32"][:, (tc % 3) * 2048:(tc % 3) * 2048 + 2048]
                if tc % 3 < 2 else io["xsl"][:, :])

    def B0_sync(sp):
        inc(sp.dma_start(out=io["idb"][:, :], in_=io["didb"]) if emit else E.x(),
            "dmisc", ("idb",), 16)
        for tc in range(NT):
            if tc >= 3:
                wge(sp, "sv", ("cast", tc - 3))
            ins = E.x()
            if emit:
                src = io["dx"][tc * 128:(tc + 1) * 128, :]
                ins = sp.dma_start(out=xstg(tc), in_=src)
            inc(ins, XSEM[tc % 3], ("x", tc), 16)

    def B0_gamma_sync(sp):
        for m in range(3):
            for p in range(NGP):
                k = m * NGP + p
                if k < 2:
                    wge(sp, "sv", ("cast", NT - 1))
                else:
                    wge(sp, "sv", ("gred", k - 2))
                ins = E.x()
                if emit:
                    g = [io["dg1"], io["dg2"], io["dg3"]][m]
                    dst = io["wf32"][:, (k % 2) * 2048:(k % 2) * 2048 + GP]
                    ins = sp.dma_start(out=dst, in_=g[:, p * GP:(p + 1) * GP])
                inc(ins, f"dg{k % 2}", ("gpiece", k), 16)

    def B0_gp_ccg(gp):
        uwait(gp, ("gmm_ev",))
        inc(gp.dma_start(out=io["ccg_in"][:, :], in_=io["tsb"][:, :])
            if emit else E.x(), "dgp", ("ccg_in",), 16)
        wge(gp, "dgp", ("ccg_in",))
        ins = E.x()
        if emit and USE_CC:
            ins = gp.collective_compute(
                "AllReduce", ALU.add, replica_groups=[list(range(NCORE))],
                ins=[io["ccg_in"].ap()], outs=[io["ccg_out"].ap()])
        elif emit:
            ins = gp.nop()
        inc(ins, "cc", ("arg",), 1)

    def B0_scalar(s):
        for tc in range(NT):
            wge(s, XSEM[tc % 3], ("x", tc))
            ins = E.x()
            if emit:
                ins = s.activation(io["attn_pre"][:, 2048:4096], xstg(tc),
                                   AF.Square, accum_out=io["lnst"][:, 2:3])
            inc(ins, "sa", ("sq", tc))
            wge(s, "sv", ("var", tc))
            ins = E.x()
            if emit:
                ins = s.activation(io["lnst"][:, 5:6], io["lnst"][:, 4:5],
                                   AF.Ln, bias=io["epsln"][:, 0:1], scale=1.0)
            inc(ins, "sa", ("lnv", tc))
            wge(s, "sa", ("lnv", tc))
            ins = E.x()
            if emit:
                ins = s.activation(io["lnst"][:, 6:7], io["lnst"][:, 5:6],
                                   AF.Exp, scale=-0.5)
            inc(ins, "sa", ("rstd", tc))

    def B0_vector(v):
        ins = E.x()
        if emit:
            v.memset(io["ones1b"][:, :], 1.0)
            v.memset(io["ones1f"][:, :], 1.0)
            v.memset(io["onecf"][:, :], 1.0)
            v.memset(io["onecb"][:, :], 1.0)
            v.memset(io["tsb"][:, :], 0.0)
            v.memset(io["gmat"][:, :], 0.0)
            ins = v.memset(io["epsln"][:, :], EPS_LN)
        inc(ins, "sv", ("memset",))
        for tc in range(NT):
            wge(v, XSEM[tc % 3], ("x", tc))
            ins = E.x()
            if emit:
                sx = xstg(tc)
                ins = v.tensor_reduce(io["lnst"][:, 0:1], sx, AX.X, ALU.add)
            inc(ins, "sv", ("r1", tc))
            wge(v, "sv", ("r1", tc))
            ins = E.x()
            if emit:
                ins = v.tensor_scalar(io["lnst"][:, 1:2], io["lnst"][:, 0:1],
                                      -1.0 / C, None, ALU.mult)
            inc(ins, "sv", ("nm", tc))
            wge(v, "sa", ("sq", tc))
            wge(v, "sv", ("nm", tc))
            ins = E.x()
            if emit:
                ins = v.tensor_tensor(io["lnst"][:, 3:4], io["lnst"][:, 1:2],
                                      io["lnst"][:, 1:2], ALU.mult)
            inc(ins, "sv", ("m2", tc))
            wge(v, "sv", ("m2", tc))
            ins = E.x()
            if emit:
                ins = v.scalar_tensor_tensor(io["lnst"][:, 4:5],
                                             io["lnst"][:, 2:3], 1.0 / C,
                                             io["lnst"][:, 3:4],
                                             ALU.mult, ALU.subtract)
            inc(ins, "sv", ("var", tc))
            wge(v, "sa", ("rstd", tc))
            ins = E.x()
            if emit:
                ins = v.tensor_tensor(io["lnst"][:, 7:8], io["lnst"][:, 1:2],
                                      io["lnst"][:, 6:7], ALU.mult)
            inc(ins, "sv", ("bb", tc))
            wge(v, "sv", ("bb", tc))
            ins = E.x()
            if emit:
                xn = io["attn_pre"][:, 0:2048]
                ins = v.tensor_scalar(xn, sx, io["lnst"][:, 6:7],
                                      io["lnst"][:, 7:8], ALU.mult, ALU.add)
            inc(ins, "sv", ("cast", tc))
            for r in range(2):
                wge(v, "spe", ("xtr", tc, r))
                ins = E.x()
                if emit:
                    dst = io["xnT"].ap().rearrange("p (cc t) -> p cc t", cc=NCC)
                    dst = dst[:, r * 8:(r + 1) * 8, tc * 128:(tc + 1) * 128]
                    src = io["ps_tb"][r].ap().rearrange("p (z t) -> p z t", z=8)
                    ins = v.tensor_copy(dst, src)
                tb_done(ins, r)
    def B0_tensor(t):
        wge(t, "dmisc", ("idb",))
        wge(t, "sv", ("memset",))
        for tc in range(NT):
            wge(t, "sv", ("cast", tc))
            for r in range(2):
                tb_wait(t, r, key=("xnt", tc, r))
                ins = E.x()
                for blk_i in range(8):
                    if emit:
                        cc = r * 8 + blk_i
                        src = io["attn_pre"][:, cc * 128:(cc + 1) * 128]
                        ins = t.transpose(
                            io["ps_tb"][r][:, blk_i * 128:(blk_i + 1) * 128],
                            src, io["idb"][:, :])
                inc(ins, "spe", ("xtr", tc, r))
            if tc < 4:
                if tc == 0:
                    wge(t, "dw0", ("kslab", 0))
                uwait(t, ("xnt", tc, 0))
                uwait(t, ("xnt", tc, 1))
                k = tc % 4
                rot_wait(t, k, key=("Kt", 0, tc))
                ins = E.x()
                for cc in range(NCC):
                    if emit:
                        lhsT = io["xnT"][:, cc * 1024 + tc * 128:
                                         cc * 1024 + (tc + 1) * 128]
                        rhs = wslab(0)[:, cc * 256:(cc + 1) * 256]
                        ins = t.matmul(io["ps_r"][k][:, 0:256], lhsT, rhs,
                                       start=(cc == 0), stop=(cc == NCC - 1))
                inc(ins, "spe", ("kproj", 0, tc))

    def wslab(i):
        return io["wsb"][:, (i % 2) * 4096:(i % 2) * 4096 + 4096]

    def B1_gp(gp):
        for part, dwn in (("k", "dwk"), ("v", "dwv")):
            for osl in range(8):
                if part == "k":
                    if osl >= 2:
                        wge(gp, "spe", ("kproj", osl - 2, NT - 1))
                else:
                    if osl <= 1:
                        wge(gp, "spe", ("kproj", 6 + osl, NT - 1))
                    else:
                        wge(gp, "spe", ("vproj", osl - 2, NT - 1))
                ins = E.x()
                if emit:
                    ins = gp.dma_start(out=wslab(osl), in_=io[dwn][osl])
                inc(ins, f"dw{osl % 2}", (part + "slab", osl), 16)

    def B1_tensor(t):
        for tc in range(4, NT):
            uwait(t, ("xnt", tc, 0))
            uwait(t, ("xnt", tc, 1))
            k = tc % 4
            rot_wait(t, k, key=("Kt", 0, tc))
            ins = E.x()
            for cc in range(NCC):
                if emit:
                    lhsT = io["xnT"][:, cc * 1024 + tc * 128:
                                     cc * 1024 + (tc + 1) * 128]
                    rhs = wslab(0)[:, cc * 256:(cc + 1) * 256]
                    ins = t.matmul(io["ps_r"][k][:, 0:256], lhsT, rhs,
                                   start=(cc == 0), stop=(cc == NCC - 1))
            inc(ins, "spe", ("kproj", 0, tc))
        tix = NT
        for osl in range(1, 8):
            wge(t, f"dw{osl % 2}", ("kslab", osl))
            for tc in range(NT):
                k = tix % 4
                rot_wait(t, k, key=("Kt", osl, tc))
                ins = E.x()
                for cc in range(NCC):
                    if emit:
                        lhsT = io["xnT"][:, cc * 1024 + tc * 128:
                                         cc * 1024 + (tc + 1) * 128]
                        rhs = wslab(osl)[:, cc * 256:(cc + 1) * 256]
                        ins = t.matmul(io["ps_r"][k][:, 0:256], lhsT, rhs,
                                       start=(cc == 0), stop=(cc == NCC - 1))
                inc(ins, "spe", ("kproj", osl, tc))
                tix += 1
        for osl in range(8):
            wge(t, f"dw{osl % 2}", ("vslab", osl))
            for tc in range(NT):
                k = tix % 4
                rot_wait(t, k, key=("Vt", osl, tc))
                ins = E.x()
                for cc in range(NCC):
                    if emit:
                        lhsT = io["xnT"][:, cc * 1024 + tc * 128:
                                         cc * 1024 + (tc + 1) * 128]
                        rhs = wslab(osl)[:, cc * 256:(cc + 1) * 256]
                        ins = t.matmul(io["ps_r"][k][:, 0:256], lhsT, rhs,
                                       start=(cc == 0), stop=(cc == NCC - 1))
                inc(ins, "spe", ("vproj", osl, tc))
                tix += 1
            if osl == 0:
                wge(t, "sv", ("gmat",))
                rot_wait(t, 0, key=("gmm_ev",))
                inc(t.matmul(io["ps_r"][0][0:1, 0:4], io["onecf"][:, 0:1],
                             io["gmat"][:, 0:4], start=True, stop=True)
                    if emit else E.x(), "spe", ("gmm",))
            uwait(t, ("Vt", osl, NT - 1))
            uwait(t, ("Kt", osl, NT - 1))
            g = osl % 2
            kb_wait(t, g, key=("kvg", osl))
            ins = E.x()
            first = True
            for hl in range(2):
                for tc in range(NT):
                    if emit:
                        base = KBUF + tc * 2048 + osl * 256 + hl * 128
                        kv_lhsT = io["arena"][:, base:base + 128]
                        vb = VST + g * 2048 + tc * 256 + hl * 128
                        vs = io["arena"][:, vb:vb + 128]
                        t.matmul(io["ps_kv"][g][:, hl * 128:(hl + 1) * 128],
                                 kv_lhsT, vs, start=first, stop=False)
                        ins = t.matmul(io["ps_kv"][g][:, 256 + hl:257 + hl],
                                       kv_lhsT, io["onecb"][:, 0:1],
                                       start=False,
                                       stop=(hl == 1 and tc == NT - 1))
                    first = False
            inc(ins, "spe", ("kvmm", osl))

    def B1_scalar(s):
        for osl in range(8):
            for tc in range(NT):
                i = osl * NT + tc
                wge(s, "spe", ("kproj", osl, tc))
                if i >= 2:
                    io2 = ((i - 2) // NT, (i - 2) % NT)
                    wge(s, "sv", ("kmn", io2[0], io2[1]))
                k = i % 4
                ins = E.x()
                if emit:
                    e = io["ev1"][:, (i % 2) * 256:(i % 2) * 256 + 256]
                    ins = s.activation(e, io["ps_r"][k][:, 0:256], AF.Exp)
                inc(ins, "sa", ("kexp", osl, tc))
        for osl in range(8):
            for tc in range(NT):
                wge(s, "spe", ("vproj", osl, tc))
                k = (64 + osl * NT + tc) % 4
                ins = E.x()
                if emit:
                    vb = VST + (osl % 2) * 2048 + tc * 256
                    ins = s.copy(io["arena"][:, vb:vb + 256],
                                 io["ps_r"][k][:, 0:256])
                rot_done(ins, k)

    def B1_vector(v):
        for osl in range(8):
            for tc in range(NT):
                i = osl * NT + tc
                wge(v, "sa", ("kexp", osl, tc))
                k = i % 4
                ins = E.x()
                if emit:
                    e = io["ev1"][:, (i % 2) * 256:(i % 2) * 256 + 256]
                    t1 = io["ev1"][:, 512 + (i % 2) * 256:512 + (i % 2) * 256 + 256]
                    ins = v.tensor_single_scalar(t1, e, 1.0, ALU.min)
                inc(ins, "sv", ("kmn", osl, tc))
                wge(v, "sv", ("kmn", osl, tc))
                ins = E.x()
                if emit:
                    base = KBUF + tc * 2048 + osl * 256
                    ins = v.scalar_tensor_tensor(io["arena"][:, base:base + 256],
                                                 io["ps_r"][k][:, 0:256], 0.0,
                                                 t1, ALU.max, ALU.add)
                rot_done(ins, k)
            for kk in range(3):
                k2 = osl * 3 + kk
                wge(v, f"dg{k2 % 2}", ("gpiece", k2))
                ins = E.x()
                if emit:
                    sg = io["wf32"][:, (k2 % 2) * 2048:(k2 % 2) * 2048 + GP]
                    ins = v.tensor_reduce(io["gacc"][:, k2:k2 + 1], sg, AX.X,
                                          ALU.add, apply_absolute_value=True)
                inc(ins, "sv", ("gred", k2))
        wge(v, "sv", ("gred", 3 * NGP - 1))
        ins = E.x()
        if emit:
            for m in range(3):
                ins = v.tensor_reduce(io["gmat"][:, m:m + 1],
                                      io["gacc"][:, m * NGP:(m + 1) * NGP],
                                      AX.X, ALU.add)
        inc(ins, "sv", ("gmat",))
        wge(v, "spe", ("gmm",))
        rot_done(v.tensor_copy(io["tsb"][0:1, 0:4], io["ps_r"][0][0:1, 0:4])
                 if emit else E.x(), 0)
        for osl in range(8):
            wge(v, "spe", ("kvmm", osl))
            g = osl % 2
            ins = E.x()
            if emit:
                v.tensor_copy(io["wf32"][:, osl * 256:(osl + 1) * 256],
                              io["ps_kv"][g][:, 0:256])
                ins = v.tensor_copy(io["ksum_sb"][:, osl * 2:osl * 2 + 2],
                                    io["ps_kv"][g][:, 256:258])
            kb_done(ins, g)
        uwait(v, ("kvg", 7))
        inc(v.tensor_copy(io["wf32"][:, 2048:2064], io["ksum_sb"][:, :])
            if emit else E.x(), "sv", ("ksfix",))

    def B1_gp2(gp):
        wge(gp, "sv", ("ksfix",))
        inc(gp.dma_start(out=io["cck_in"][:, :], in_=io["wf32"][:, 0:2064])
            if emit else E.x(), "dgp", ("cck_in",), 16)
        wge(gp, "dgp", ("cck_in",))
        ins = E.x()
        if emit and USE_CC:
            ins = gp.collective_compute(
                "AllReduce", ALU.add, replica_groups=[list(range(NCORE))],
                ins=[io["cck_in"].ap()], outs=[io["cck_out"].ap()])
        elif emit:
            ins = gp.nop()
        inc(ins, "cc", ("ark",), 1)

    def wqslab(h):
        return io["wsb"][:, (h % 4) * 2048:(h % 4) * 2048 + 2048]

    def qsl(h, th):
        return io["arena"][:, h * 1024 + th * 512:h * 1024 + (th + 1) * 512]

    WQSEM = ["dw0", "dw1", "dq0", "dq1"]

    def B2_gpA(gp):
        for h in range(4):
            wge(gp, "spe", ("vproj", 7, NT - 1))
            ins = E.x()
            if emit:
                ins = gp.dma_start(out=wqslab(h), in_=io["dwq"][h])
            inc(ins, WQSEM[h % 4], ("wqslab", h), 16)

    def B2_gp(gp):
        for h in range(4, 6):
            wge(gp, "spe", ("qproj", h - 4, 1))
            ins = E.x()
            if emit:
                ins = gp.dma_start(out=wqslab(h), in_=io["dwq"][h])
            inc(ins, WQSEM[h % 4], ("wqslab", h), 16)
        wge(gp, "cc", ("arg",))
        inc(gp.dma_start(out=io["gsum"][:, :], in_=io["ccg_out"][:, :])
            if emit else E.x(), "dgp", ("gsum",), 16)
        for h in range(6, H):
            wge(gp, "spe", ("qproj", h - 4, 1))
            ins = E.x()
            if emit:
                ins = gp.dma_start(out=wqslab(h), in_=io["dwq"][h])
            inc(ins, WQSEM[h % 4], ("wqslab", h), 16)
        wge(gp, "cc", ("ark",))
        ins1 = E.x()
        ins2 = E.x()
        if emit:
            ins1 = gp.dma_start(out=io["arena"][:, KVBF:KVBF + 2048],
                                in_=io["cck_out"][:, 0:2048])
            ins2 = gp.dma_start(out=io["ksum_bf"][:, :],
                                in_=io["cck_out"][:, 2048:2064])
        inc(ins1, "dgp", None, 16)
        inc(ins2, "dgp", ("kvbf",), 16)

    def B2_tensor(t):
        for h in range(H):
            wge(t, WQSEM[h % 4], ("wqslab", h))
            for th in range(2):
                k = (h * 2 + th) % 4
                rot_wait(t, k, key=("Qt", h, th))
                ins = E.x()
                for cc in range(NCC):
                    if emit:
                        lhsT = wqslab(h)[:, cc * 128:(cc + 1) * 128]
                        rhs = io["xnT"][:, cc * 1024 + th * 512:
                                        cc * 1024 + (th + 1) * 512]
                        ins = t.matmul(io["ps_r"][k][:, :], lhsT, rhs,
                                       start=(cc == 0), stop=(cc == NCC - 1))
                inc(ins, "spe", ("qproj", h, th))
            if h == 8:
                wge(t, "sv", ("tcalc",))
                rot_wait(t, 0, key=("tbc_ev",))
                inc(t.matmul(io["ps_r"][0][:, 0:4], io["ones1f"][0:1, :],
                             io["tsb"][0:1, 0:4], start=True, stop=True)
                    if emit else E.x(), "spe", ("tbc",))
        wge(t, "dgp", ("kvbf",))
        for h in range(H):
            for th in range(2):
                uwait(t, ("Qt", h, th))
                kb_wait(t, th, key=("dent", h, th))
                ins = E.x()
                if emit:
                    ins = t.matmul(io["ps_kv"][th][0:1, :],
                                   io["ksum_bf"][:, h:h + 1], qsl(h, th),
                                   start=True, stop=True)
                inc(ins, "spe", ("den", h, th))
            for th in range(2):
                ka = (h * 2 + th) % 4
                rot_wait(t, ka, key=("att", h, th))
                ins = E.x()
                if emit:
                    ins = t.matmul(io["ps_r"][ka][:, :],
                                   io["arena"][:, KVBF + h * 128:
                                               KVBF + (h + 1) * 128],
                                   qsl(h, th), start=True, stop=True)
                inc(ins, "spe", ("attn", h, th))
            for th in range(2):
                wge(t, "sa", ("z", h, th))
                kz = (h * 2 + th + 2) % 4
                rot_wait(t, kz, key=("zbt", h, th))
                ins = E.x()
                if emit:
                    ins = t.matmul(io["ps_r"][kz][:, :], io["ones1f"][0:1, :],
                                   io["xsl"][0:1, th * 512:(th + 1) * 512],
                                   start=True, stop=True)
                inc(ins, "spe", ("zb", h, th))

    def B2_scalar(s):
        for h in range(H):
            for th in range(2):
                wge(s, "spe", ("qproj", h, th))
                if h >= 1:
                    wge(s, "sv", ("qmn", h - 1, th))
                k = (h * 2 + th) % 4
                ins = E.x()
                if emit:
                    e = io["ev1"][:, th * 512:(th + 1) * 512]
                    ins = s.activation(e, io["ps_r"][k][:, :], AF.Exp)
                inc(ins, "sa", ("qexp", h, th))
        for h in range(H):
            for th in range(2):
                uwait(s, ("dent", h, th))
                ins = E.x()
                if emit:
                    zs = io["xsl"][0:1, th * 512:(th + 1) * 512]
                    ins = s.activation(zs, zs, AF.Ln)
                inc(ins, "sa", ("lnz", h, th))
                wge(s, "sa", ("lnz", h, th))
                ins = E.x()
                if emit:
                    zs = io["xsl"][0:1, th * 512:(th + 1) * 512]
                    ins = s.activation(zs, zs, AF.Exp, scale=-1.0)
                inc(ins, "sa", ("z", h, th))
            for th in range(2):
                wge(s, "spe", ("attn", h, th))
                if h >= 1:
                    uwait(s, ("zbt", h - 1, th))
                ka = (h * 2 + th) % 4
                ins = E.x()
                if emit:
                    ins = s.copy(io["ev2"][:, th * 512:(th + 1) * 512],
                                 io["ps_r"][ka][:, :])
                rot_done(ins, ka)

    def B2_vector(v):
        for h in range(H):
            if h == 6:
                wge(v, "dgp", ("gsum",))
                inc(v.tensor_scalar(io["tsb"][0:1, 0:4], io["gsum"][0:1, 0:4],
                                    0.5 / NW, 0.5 * EPS_Q, ALU.mult, ALU.add)
                    if emit else E.x(), "sv", ("tcalc",))
            if h == 9:
                wge(v, "spe", ("tbc",))
                if emit:
                    inc(v.tensor_copy(io["tbc"][:, 0:4],
                                      io["ps_r"][0][:, 0:4]), "sv", ("tbcast",))
                    rot_done(v.tensor_scalar(io["tbc"][:, 4:8],
                                             io["ps_r"][0][:, 0:4],
                                             -1.0, None, ALU.mult), 0)
                else:
                    inc(E.x(), "sv", ("tbcast",))
                    rot_done(E.x(), 0)
            for th in range(2):
                wge(v, "sa", ("qexp", h, th))
                k = (h * 2 + th) % 4
                ins = E.x()
                if emit:
                    e = io["ev1"][:, th * 512:(th + 1) * 512]
                    t1 = io["ev2"][:, th * 512:(th + 1) * 512]
                    ins = v.tensor_single_scalar(t1, e, 1.0, ALU.min)
                inc(ins, "sv", ("qmn", h, th))
                wge(v, "sv", ("qmn", h, th))
                ins = E.x()
                if emit:
                    t1 = io["ev2"][:, th * 512:(th + 1) * 512]
                    ins = v.scalar_tensor_tensor(qsl(h, th), io["ps_r"][k][:, :],
                                                 0.0, t1, ALU.max, ALU.add)
                rot_done(ins, k)
        for h in range(H):
            for th in range(2):
                wge(v, "spe", ("den", h, th))
                ins = E.x()
                if emit:
                    zs = io["xsl"][0:1, th * 512:(th + 1) * 512]
                    ins = v.tensor_single_scalar(zs, io["ps_kv"][th][0:1, :],
                                                 EPS_Z, ALU.add)
                kb_done(ins, th)
            for th in range(2):
                uwait(v, ("att", h, th))
                wge(v, "spe", ("zb", h, th))
                kz = (h * 2 + th + 2) % 4
                ins = E.x()
                if emit:
                    dst = io["attn_pre"][:, h * 1024 + th * 512:
                                         h * 1024 + (th + 1) * 512]
                    ins = v.tensor_tensor(dst,
                                          io["ev2"][:, th * 512:(th + 1) * 512],
                                          io["ps_r"][kz][:, :], ALU.mult)
                rot_done(ins, kz)

    def B3_sync(sp):
        wge(sp, "dgp", ("cck_in",))
        for j in range(NHC):
            for m in range(2):
                if j >= 1:
                    wge(sp, "sv", ("quant", j - 1, m))
                ins = E.x()
                if emit:
                    dw = io["dw1"] if m == 0 else io["dw2"]
                    dst = io["wf32"][:, m * 2048:(m + 1) * 2048]
                    ins = sp.dma_start(out=dst, in_=dw[j])
                inc(ins, f"ds{m}", ("wst", j, m), 16)

    def B3_vector(v):
        for j in range(NHC):
            for m in range(2):
                wge(v, f"ds{m}", ("wst", j, m))
                if j <= 1:
                    wge(v, "spe", ("qproj", H - 1, 1))
                if j >= 2:
                    wge(v, "spe", ("x12", j - 2, m, 1))
                ins = E.x()
                if emit:
                    wsrc = io["wf32"][:, m * 2048:(m + 1) * 2048]
                    qoff = (m * 2 + j % 2) * 2048
                    qdst = io["wsb"][:, qoff:qoff + 2048]
                    ins = v.tensor_scalar(qdst, wsrc, io["tbc"][:, 4 + m:5 + m],
                                          -1.0, ALU.is_lt, ALU.mult)
                inc(ins, "sv", ("q1", j, m))
                wge(v, "sv", ("q1", j, m))
                ins = E.x()
                if emit:
                    ins = v.scalar_tensor_tensor(qdst, wsrc,
                                                 io["tbc"][:, m:m + 1],
                                                 qdst, ALU.is_gt, ALU.add)
                inc(ins, "sv", ("quant", j, m))
            for th in range(2):
                uwait(v, ("x12t", j, 0, th))
                wge(v, "spe", ("x12", j, 1, th))
                k2 = (j * 4 + 2 + th) % 4
                ins = E.x()
                if emit:
                    s1 = io["ev1"][:, th * 512:(th + 1) * 512]
                    dst = io["arena"][:, j * 1024 + th * 512:
                                      j * 1024 + (th + 1) * 512]
                    ins = v.tensor_tensor(dst, s1, io["ps_r"][k2][:, :],
                                          ALU.mult)
                rot_done(ins, k2)

    def B3_scalar(s):
        for j in range(NHC):
            for th in range(2):
                wge(s, "spe", ("x12", j, 0, th))
                if j >= 1:
                    uwait(s, ("x12t", j - 1, 1, th))
                k1 = (j * 4 + th) % 4
                ins = E.x()
                if emit:
                    ins = s.activation(io["ev1"][:, th * 512:(th + 1) * 512],
                                       io["ps_r"][k1][:, :], AF.Sigmoid)
                rot_done(ins, k1)

    def B3_tensor(t):
        for j in range(NHC):
            for m in range(2):
                wge(t, "sv", ("quant", j, m))
                for th in range(2):
                    k = (j * 4 + m * 2 + th) % 4
                    rot_wait(t, k, key=("x12t", j, m, th))
                    ins = E.x()
                    for cc in range(NCC):
                        if emit:
                            qoff = (m * 2 + j % 2) * 2048
                            lhsT = io["wsb"][:, qoff + cc * 128:
                                             qoff + (cc + 1) * 128]
                            rhs = io["xnT"][:, cc * 1024 + th * 512:
                                            cc * 1024 + (th + 1) * 512]
                            ins = t.matmul(io["ps_r"][k][:, :], lhsT, rhs,
                                           start=(cc == 0),
                                           stop=(cc == NCC - 1))
                    inc(ins, "spe", ("x12", j, m, th))

    def B4_gp(gp):
        wge(gp, "spe", ("x12", NHC - 1, 1, 1))
        for oc in range(B4_NOC):
            if oc >= 2:
                wge(gp, "spe", ("wo_done", oc - 2))
            ins = E.x()
            if emit:
                dst = io["wsb"][:, (oc % 2) * 2048:(oc % 2 + 1) * 2048]
                ins = gp.dma_start(out=dst, in_=io["dwo"][oc])
            inc(ins, f"dw{oc % 2}", ("woslab", oc), 16)

    def B4_sync(sp):
        wge(sp, "sv", ("quant", NHC - 1, 1))
        wge(sp, "spe", ("zb", H - 1, 1))
        for oc in range(B4_NOC):
            if oc >= 2:
                wge(sp, f"do{oc % 2}", ("outdma", oc - 2))
            ins = E.x()
            if emit:
                dst = io["xsl"][:, (oc % 2) * 1024:(oc % 2 + 1) * 1024]
                ins = sp.dma_start(out=dst, in_=io["dxcs"][oc])
            inc(ins, f"dx{oc % 2}", ("xslab", oc), 16)
            for pi in range(4):
                gpi = oc * 4 + pi
                if gpi >= 2:
                    poc, ppi = divmod(gpi - 2, 4)
                    wge(sp, "sv", ("w3q", poc, ppi))
                ins = E.x()
                if emit:
                    dst = io["wf32"][:, (pi % 2) * 1408:(pi % 2) * 1408 + 1408]
                    ins = sp.dma_start(out=dst,
                                       in_=io["dw3"][oc][:, pi * 1408:(pi + 1) * 1408])
                inc(ins, f"ds{pi % 2}", ("w3st", oc, pi), 16)
            if oc >= 1:
                uwait(sp, ("otr", oc - 1, 0))
                uwait(sp, ("otr", oc - 1, 1))
                ins = E.x()
                if emit:
                    dst = io["dout"].rearrange("(tc p) c -> p tc c", p=128)
                    dst = dst[:, :, (oc - 1) * 128:oc * 128]
                    src = io["xsl"][:, ((oc - 1) % 2) * 1024:((oc - 1) % 2 + 1) * 1024]
                    src = src.rearrange("p (tc c) -> p tc c", tc=NT)
                    ins = sp.dma_start(out=dst, in_=src)
                inc(ins, f"do{(oc - 1) % 2}", ("outdma", oc - 1), 16)
        oc = B4_NOC - 1
        uwait(sp, ("otr", oc, 0))
        uwait(sp, ("otr", oc, 1))
        ins = E.x()
        if emit:
            dst = io["dout"].rearrange("(tc p) c -> p tc c", p=128)
            dst = dst[:, :, oc * 128:(oc + 1) * 128]
            src = io["xsl"][:, (oc % 2) * 1024:(oc % 2 + 1) * 1024]
            src = src.rearrange("p (tc c) -> p tc c", tc=NT)
            ins = sp.dma_start(out=dst, in_=src)
        inc(ins, f"do{oc % 2}", ("outdma", oc), 16)
        last_even = (B4_NOC - 1) // 2 * 2
        last_odd = (B4_NOC - 2) // 2 * 2 + 1
        wge(sp, "do0", ("outdma", last_even))
        if B4_NOC >= 2:
            wge(sp, "do1", ("outdma", last_odd))

    def B4_vector(v):
        for oc in range(B4_NOC):
            for pi in range(4):
                wge(v, f"ds{pi % 2}", ("w3st", oc, pi))
                if oc == 0 and pi == 0:
                    wge(v, "spe", ("x12", NHC - 1, 1, 1))
                if oc >= 2 and pi == 0:
                    wge(v, "spe", ("omm", oc - 2, 1))
                ins = E.x()
                if emit:
                    wsrc = io["wf32"][:, (pi % 2) * 1408:(pi % 2) * 1408 + 1408]
                    qdst = io["xnT"][:, (oc % 2) * 5632 + pi * 1408:
                                     (oc % 2) * 5632 + (pi + 1) * 1408]
                    ins = v.tensor_scalar(qdst, wsrc, io["tbc"][:, 6:7], -1.0,
                                          ALU.is_lt, ALU.mult)
                inc(ins, "sv", ("q31", oc, pi))
                wge(v, "sv", ("q31", oc, pi))
                ins = E.x()
                if emit:
                    wsrc = io["wf32"][:, (pi % 2) * 1408:(pi % 2) * 1408 + 1408]
                    qdst = io["xnT"][:, (oc % 2) * 5632 + pi * 1408:
                                     (oc % 2) * 5632 + (pi + 1) * 1408]
                    ins = v.scalar_tensor_tensor(qdst, wsrc, io["tbc"][:, 2:3],
                                                 qdst, ALU.is_gt, ALU.add)
                inc(ins, "sv", ("w3q", oc, pi))
            if oc >= 1:
                for th in range(2):
                    wge(v, "spe", ("otr", oc - 1, th))
                    wge(v, f"dx{(oc - 1) % 2}", ("xslab", oc - 1))
                    tb = th % 2
                    ins = E.x()
                    if emit:
                        xs = io["xsl"][:, ((oc - 1) % 2) * 1024 + th * 512:
                                       ((oc - 1) % 2) * 1024 + (th + 1) * 512]
                        ins = v.tensor_tensor(xs, io["ps_tb"][tb][:, 0:512],
                                              xs, ALU.add)
                    tb_done(ins, tb)
        oc = B4_NOC - 1
        for th in range(2):
            wge(v, "spe", ("otr", oc, th))
            wge(v, f"dx{oc % 2}", ("xslab", oc))
            tb = th % 2
            ins = E.x()
            if emit:
                xs = io["xsl"][:, (oc % 2) * 1024 + th * 512:
                               (oc % 2) * 1024 + (th + 1) * 512]
                ins = v.tensor_tensor(xs, io["ps_tb"][tb][:, 0:512],
                                      xs, ALU.add)
            tb_done(ins, tb)

    def B4_scalar(s):
        for oc in range(B4_NOC):
            for th in range(2):
                wge(s, "spe", ("omm", oc, th))
                if oc >= 1:
                    wge(s, "spe", ("otr", oc - 1, th))
                k = (oc * 2 + th) % 4
                ins = E.x()
                if emit:
                    ins = s.copy(io["ev2"][:, th * 512:(th + 1) * 512],
                                 io["ps_r"][k][:, :])
                rot_done(ins, k)

    def emit_tr(t, oc):
        for th in range(2):
            uwait(t, ("ot", oc, th))
            tb = th % 2
            tb_wait(t, tb, key=("otr", oc, th))
            ins = E.x()
            for blk_i in range(4):
                if emit:
                    src = io["ev2"][:, th * 512 + blk_i * 128:
                                    th * 512 + (blk_i + 1) * 128]
                    ins = t.transpose(
                        io["ps_tb"][tb][:, blk_i * 128:(blk_i + 1) * 128],
                        src, io["idb"][:, :])
            inc(ins, "spe", ("otr", oc, th))

    def B4_tensor(t):
        for oc in range(B4_NOC):
            wge(t, f"dw{oc % 2}", ("woslab", oc))
            ks = [(oc * 2) % 4, (oc * 2 + 1) % 4]
            for th in range(2):
                rot_wait(t, ks[th], key=("ot", oc, th))
            ins = E.x()
            for th in range(2):
                for cc in range(NCC):
                    if emit:
                        lhsT = io["wsb"][:, (oc % 2) * 2048 + cc * 128:
                                         (oc % 2) * 2048 + (cc + 1) * 128]
                        rhs = io["attn_pre"][:, cc * 1024 + th * 512:
                                             cc * 1024 + (th + 1) * 512]
                        ins = t.matmul(io["ps_r"][ks[th]][:, :], lhsT, rhs,
                                       start=(cc == 0), stop=False)
            inc(ins, "spe", ("wo_done", oc))
            if oc >= 1:
                emit_tr(t, oc - 1)
            wge(t, "sv", ("w3q", oc, 3))
            for th in range(2):
                ins = E.x()
                for hc in range(NHC):
                    if emit:
                        qoff = (oc % 2) * 5632 + hc * 128
                        lhsT = io["xnT"][:, qoff:qoff + 128]
                        rhs = io["arena"][:, hc * 1024 + th * 512:
                                          hc * 1024 + (th + 1) * 512]
                        ins = t.matmul(io["ps_r"][ks[th]][:, :], lhsT, rhs,
                                       start=False, stop=(hc == NHC - 1))
                inc(ins, "spe", ("omm", oc, th))
        emit_tr(t, B4_NOC - 1)

    nph = NPHASES
    if emit:
        if nph >= 4:
            with nc.Block("bmain") as blk:
                blk.sync(B0_sync)
                blk.tensor(B0_tensor)
                blk.tensor(B1_tensor)
                blk.tensor(B2_tensor)
                blk.tensor(B3_tensor)
                blk.tensor(B4_tensor)
                blk.gpsimd(B1_gp)
                blk.gpsimd(B2_gpA)
                blk.gpsimd(B1_gp2)
                blk.gpsimd(B0_gp_ccg)
                blk.gpsimd(B2_gp)
                blk.gpsimd(B4_gp)
                blk.sync(B0_gamma_sync)
                blk.sync(B3_sync)
                blk.sync(B4_sync)
                blk.scalar(B0_scalar)
                blk.scalar(B1_scalar)
                blk.scalar(B2_scalar)
                blk.scalar(B3_scalar)
                blk.scalar(B4_scalar)
                blk.vector(B0_vector)
                blk.vector(B1_vector)
                blk.vector(B2_vector)
                blk.vector(B3_vector)
                blk.vector(B4_vector)

    else:
        if nph >= 5:
            B0_sync(E); B0_tensor(E); B1_tensor(E); B2_tensor(E); B3_tensor(E)
            B4_tensor(E)
            B1_gp(E); B2_gpA(E); B1_gp2(E); B0_gp_ccg(E); B2_gp(E); B4_gp(E)
            B0_gamma_sync(E); B3_sync(E); B4_sync(E)
            B0_scalar(E); B1_scalar(E); B2_scalar(E); B3_scalar(E); B4_scalar(E)
            B0_vector(E); B1_vector(E); B2_vector(E); B3_vector(E); B4_vector(E)
    return mil


def make_nc():
    mil = build_program(emit=False)
    nc = bass.Bass("TRN2", target_bir_lowering=False, debug=False)
    build_program(emit=True, nc=nc, mil=mil)
    return nc


def kernel(L0_attn_x, L0_attn_cos, L0_attn_sin, L0_attn_Wq, L0_attn_Wk,
           L0_attn_Wv, L0_attn_Wo, L0_ffn_w1, L0_ffn_w2, L0_ffn_w3):
    assert np.asarray(L0_attn_cos).shape == (4096, 128)
    assert np.asarray(L0_attn_sin).shape == (4096, 128)
    x = np.ascontiguousarray(
        np.asarray(L0_attn_x, dtype=np.float32).reshape(8192, C))
    wqT = np.asarray(L0_attn_Wq, dtype=np.float32).T
    wkT = np.asarray(L0_attn_Wk, dtype=np.float32).T
    wvT = np.asarray(L0_attn_Wv, dtype=np.float32).T
    woT = np.asarray(L0_attn_Wo, dtype=np.float32).T
    w1T = np.asarray(L0_ffn_w1, dtype=np.float32).T
    w2T = np.asarray(L0_ffn_w2, dtype=np.float32).T
    w3T = np.asarray(L0_ffn_w3, dtype=np.float32).T
    idb = np.eye(128).astype(ml_dtypes.bfloat16)

    def slabify(wt, nslab, width):
        kk = wt.shape[0] // 128
        a = wt.reshape(kk, 128, nslab, width).transpose(2, 1, 0, 3)
        return np.ascontiguousarray(a.reshape(nslab, 128, kk * width))

    wqS = slabify(wqT, H, 128)
    wkS = slabify(wkT, 8, 256)
    wvS = slabify(wvT, 8, 256)
    woS = slabify(woT, NCC, 128)
    w1S = slabify(w1T, NHC, 128)
    w2S = slabify(w2T, NHC, 128)
    w3S = slabify(w3T, NCC, 128)

    nr12 = C // NCORE
    nr3 = HF // NCORE
    in_maps = []
    for c in range(NCORE):
        g1 = np.ascontiguousarray(
            w1T[c * nr12:(c + 1) * nr12, :]).reshape(128, GCOLS)
        g2 = np.ascontiguousarray(
            w2T[c * nr12:(c + 1) * nr12, :]).reshape(128, GCOLS)
        g3 = np.ascontiguousarray(
            w3T[c * nr3:(c + 1) * nr3, :]).reshape(128, GCOLS)
        xc = np.ascontiguousarray(x[c * TC:(c + 1) * TC, :])
        xcs = np.ascontiguousarray(
            xc.reshape(NT, 128, NCC, 128).transpose(2, 1, 0, 3)).reshape(
                NCC, 128, NT * 128)
        in_maps.append({
            "x": xc, "xcs": xcs,
            "wqS": wqS, "wkS": wkS, "wvS": wvS, "woS": woS,
            "w1S": w1S, "w2S": w2S, "w3S": w3S,
            "g1": g1, "g2": g2, "g3": g3,
            "idb": idb,
        })

    nc = make_nc()
    res = run_bass_kernel_spmd(nc, in_maps, core_ids=list(range(NCORE)))
    out = np.concatenate([np.asarray(r["out"]) for r in res.results], axis=0)
    return out.reshape(1, 8192, C).astype(np.float32)


# version 22 (final)
# speedup vs baseline: 1.0205x; 1.0016x over previous
import numpy as np
import ml_dtypes
import concourse.bass as bass
import concourse.mybir as mybir
from concourse.bass_utils import run_bass_kernel_spmd

F32 = mybir.dt.float32
BF16 = mybir.dt.bfloat16
AF = mybir.ActivationFunctionType
ALU = mybir.AluOpType
AX = mybir.AxisListType

NCORE = 8
TC = 1024
NT = 8
C = 2048
NCC = 16
H = 16
NHC = 44
HF = 5632
NW = HF * C
EPS_LN = 1e-5
EPS_Q = 1e-8
EPS_Z = 1e-6

GCOLS = NW // NCORE // 128
NGP = 8
GP = GCOLS // NGP

NPHASES = 5
USE_CC = 1
B4_NOC = 16
DEBUG_DUMP = 0

KBUF = 0
VST = 16384
KVBF = 20480


class Mil:
    def __init__(self):
        self.c = {}
        self.m = {}

    def bump(self, sem, key=None, k=1):
        self.c[sem] = self.c.get(sem, 0) + k
        if key is not None:
            kk = (sem, key)
            assert kk not in self.m, kk
            self.m[kk] = self.c[sem]
        return self.c[sem]

    def at(self, sem, key):
        return self.m[(sem, key)]


class _DI:
    def then_inc(self, *a, **k):
        return self


class _DummyEng:
    def __getattr__(self, name):
        return lambda *a, **k: _DI()


SEM_NAMES = [
    "dgp", "dmisc", "dx0", "dx1", "dg0", "dg1", "dw0", "dw1", "dk2",
    "dq0", "dq1", "ds0", "ds1", "do0", "do1", "sv", "sa", "spe", "cc",
    "rb0", "rb1", "rb2", "rb3", "tb0", "tb1", "kb0", "kb1",
]


def build_program(emit, nc=None, mil=None):
    if mil is None:
        mil = Mil()

    class B:
        pass

    b = B()
    b.emit = emit
    E = _DummyEng()

    def inc(instr, sem, key=None, k=1):
        if emit:
            instr.then_inc(b.sems[sem], k)
        else:
            mil.bump(sem, key, k)
        return instr

    def wge(eng, sem, key):
        if emit:
            eng.wait_ge(b.sems[sem], mil.at(sem, key))

    def wval(eng, sem, val):
        if emit:
            eng.wait_ge(b.sems[sem], val)

    io = {}
    if emit:
        b.sems = {n: nc.alloc_semaphore(n) for n in SEM_NAMES}

        io["dx"] = nc.dram_tensor("x", [TC, C], F32, kind="ExternalInput").ap()
        io["dxcs"] = nc.dram_tensor("xcs", [NCC, 128, 1024], F32,
                                    kind="ExternalInput").ap()
        io["dwq"] = nc.dram_tensor("wqS", [H, 128, 2048], F32,
                                   kind="ExternalInput").ap()
        io["dwk"] = nc.dram_tensor("wkS", [8, 128, 4096], F32,
                                   kind="ExternalInput").ap()
        io["dwv"] = nc.dram_tensor("wvS", [8, 128, 4096], F32,
                                   kind="ExternalInput").ap()
        io["dwo"] = nc.dram_tensor("woS", [NCC, 128, 2048], F32,
                                   kind="ExternalInput").ap()
        io["dw1"] = nc.dram_tensor("w1S", [NHC, 128, 2048], F32,
                                   kind="ExternalInput").ap()
        io["dw2"] = nc.dram_tensor("w2S", [NHC, 128, 2048], F32,
                                   kind="ExternalInput").ap()
        io["dw3"] = nc.dram_tensor("w3S", [NCC, 128, 5632], F32,
                                   kind="ExternalInput").ap()
        io["dg1"] = nc.dram_tensor("g1", [128, GCOLS], F32, kind="ExternalInput").ap()
        io["dg2"] = nc.dram_tensor("g2", [128, GCOLS], F32, kind="ExternalInput").ap()
        io["dg3"] = nc.dram_tensor("g3", [128, GCOLS], F32, kind="ExternalInput").ap()
        io["didb"] = nc.dram_tensor("idb", [128, 128], BF16, kind="ExternalInput").ap()
        io["dout"] = nc.dram_tensor("out", [TC, C], F32, kind="ExternalOutput").ap()
        if DEBUG_DUMP:
            io["dbg_tbc"] = nc.dram_tensor("dbg_tbc", [128, 8], F32,
                                           kind="ExternalOutput").ap()
            io["dbg_ks"] = nc.dram_tensor("dbg_ks", [128, 16], BF16,
                                          kind="ExternalOutput").ap()
            io["dbg_ap"] = nc.dram_tensor("dbg_ap", [128, 2048], BF16,
                                          kind="ExternalOutput").ap()
            io["dbg_hid"] = nc.dram_tensor("dbg_hid", [128, 2048], BF16,
                                           kind="ExternalOutput").ap()
            io["dbg_xnt"] = nc.dram_tensor("dbg_xnt", [128, 2048], BF16,
                                           kind="ExternalOutput").ap()
            io["dbg_kv"] = nc.dram_tensor("dbg_kv", [128, 2048], BF16,
                                          kind="ExternalOutput").ap()

        io["ccg_in"] = nc.dram_tensor("ccg_in", [1, 4], F32)
        io["ccg_out"] = nc.dram_tensor("ccg_out", [1, 4], F32)
        io["cck_in"] = nc.dram_tensor("cck_in", [128, 2064], F32)
        io["cck_out"] = nc.dram_tensor("cck_out", [128, 2064], F32)

        A = nc.alloc_sbuf_tensor
        io["xnT"] = A("xnT", [128, 16384], BF16)
        io["arena"] = A("arena", [128, NHC * 1024], BF16)
        io["attn_pre"] = A("attn_pre", [128, 16384], BF16)
        io["wsb"] = A("wsb", [128, 9728], BF16)
        io["wf32"] = A("wf32", [128, 4096], F32)
        io["qh"] = A("qh", [128, 2048], BF16)
        io["ev1"] = A("ev1", [128, 1024], BF16)
        io["ev2"] = A("ev2", [128, 1024], BF16)
        io["xsl"] = A("xsl", [128, 2048], F32)
        io["ksum_sb"] = A("ksum_sb", [128, 16], F32)
        io["ksum_bf"] = A("ksum_bf", [128, 16], BF16)
        io["lnst"] = A("lnst", [128, 8], F32)
        io["gacc"] = A("gacc", [128, 24], F32)
        io["gmat"] = A("gmat", [128, 4], F32)
        io["tsb"] = A("tsb", [1, 4], F32)
        io["gsum"] = A("gsum", [1, 4], F32)
        io["tbc"] = A("tbc", [128, 8], F32)
        io["idb"] = A("idb_sb", [128, 128], BF16)
        io["ones1b"] = A("ones1b", [1, 128], BF16)
        io["ones1f"] = A("ones1f", [1, 128], F32)
        io["onecf"] = A("onecf", [128, 1], F32)
        io["onecb"] = A("onecb", [128, 1], BF16)
        io["epsln"] = A("epsln", [128, 1], F32)

        P = nc.alloc_psum_tensor
        io["ps_r"] = [P(f"ps_r{i}", [128, 512], F32) for i in range(4)]
        io["ps_tb"] = [P("ps_tb0", [128, 1024], BF16),
                       P("ps_tb1", [128, 1024], BF16)]
        io["ps_kv"] = [P("ps_kv0", [128, 512], F32), P("ps_kv1", [128, 512], F32)]

    rb_use = [0, 0, 0, 0]
    tb_use = [0, 0]
    kb_use = [0, 0]
    U = {}

    def rot_wait(eng, k, key=None):
        if rb_use[k] > 0:
            wval(eng, f"rb{k}", rb_use[k])
        rb_use[k] += 1
        if key is not None:
            U[key] = (f"rb{k}", rb_use[k])
        return rb_use[k]

    def rot_done(instr, k):
        inc(instr, f"rb{k}")

    def uwait(eng, key):
        sem, val = U[key]
        wval(eng, sem, val)

    def tb_wait(eng, k, key=None):
        if tb_use[k] > 0:
            wval(eng, f"tb{k}", tb_use[k])
        tb_use[k] += 1
        if key is not None:
            U[key] = (f"tb{k}", tb_use[k])
        return tb_use[k]

    def tb_done(instr, k):
        inc(instr, f"tb{k}")

    def kb_wait(eng, k, key=None):
        if kb_use[k] > 0:
            wval(eng, f"kb{k}", kb_use[k])
        kb_use[k] += 1
        if key is not None:
            U[key] = (f"kb{k}", kb_use[k])
        return kb_use[k]

    def kb_done(instr, k):
        inc(instr, f"kb{k}")

    XSEM = ["dx0", "dx1", "dg0"]

    def xstg(tc):
        return (io["wf32"][:, (tc % 3) * 2048:(tc % 3) * 2048 + 2048]
                if tc % 3 < 2 else io["xsl"][:, :])

    def B0_sync(sp):
        inc(sp.dma_start(out=io["idb"][:, :], in_=io["didb"]) if emit else E.x(),
            "dmisc", ("idb",), 16)
        for tc in range(NT):
            if tc >= 3:
                wge(sp, "sv", ("cast", tc - 3))
            ins = E.x()
            if emit:
                src = io["dx"][tc * 128:(tc + 1) * 128, :]
                ins = sp.dma_start(out=xstg(tc), in_=src)
            inc(ins, XSEM[tc % 3], ("x", tc), 16)

    def B0_gamma_sync(sp):
        for m in range(3):
            for p in range(NGP):
                k = m * NGP + p
                if k < 2:
                    wge(sp, "sv", ("cast", NT - 1))
                else:
                    wge(sp, "sv", ("gred", k - 2))
                ins = E.x()
                if emit:
                    g = [io["dg1"], io["dg2"], io["dg3"]][m]
                    dst = io["wf32"][:, (k % 2) * 2048:(k % 2) * 2048 + GP]
                    ins = sp.dma_start(out=dst, in_=g[:, p * GP:(p + 1) * GP])
                inc(ins, f"dg{k % 2}", ("gpiece", k), 16)

    def B0_gp_ccg(gp):
        uwait(gp, ("gmm_ev",))
        inc(gp.dma_start(out=io["ccg_in"][:, :], in_=io["tsb"][:, :])
            if emit else E.x(), "dgp", ("ccg_in",), 16)
        wge(gp, "dgp", ("ccg_in",))
        ins = E.x()
        if emit and USE_CC:
            ins = gp.collective_compute(
                "AllReduce", ALU.add, replica_groups=[list(range(NCORE))],
                ins=[io["ccg_in"].ap()], outs=[io["ccg_out"].ap()])
        elif emit:
            ins = gp.nop()
        inc(ins, "cc", ("arg",), 1)

    def B0_scalar(s):
        for tc in range(NT):
            wge(s, XSEM[tc % 3], ("x", tc))
            ins = E.x()
            if emit:
                ins = s.activation(io["attn_pre"][:, 6144:8192], xstg(tc),
                                   AF.Square, accum_out=io["lnst"][:, 2:3])
            inc(ins, "sa", ("sq", tc))
            wge(s, "sv", ("var", tc))
            ins = E.x()
            if emit:
                ins = s.activation(io["lnst"][:, 5:6], io["lnst"][:, 4:5],
                                   AF.Ln, bias=io["epsln"][:, 0:1], scale=1.0)
            inc(ins, "sa", ("lnv", tc))
            wge(s, "sa", ("lnv", tc))
            ins = E.x()
            if emit:
                ins = s.activation(io["lnst"][:, 6:7], io["lnst"][:, 5:6],
                                   AF.Exp, scale=-0.5)
            inc(ins, "sa", ("rstd", tc))

    def B0_vector(v):
        ins = E.x()
        if emit:
            v.memset(io["ones1b"][:, :], 1.0)
            v.memset(io["ones1f"][:, :], 1.0)
            v.memset(io["onecf"][:, :], 1.0)
            v.memset(io["onecb"][:, :], 1.0)
            v.memset(io["tsb"][:, :], 0.0)
            v.memset(io["gmat"][:, :], 0.0)
            ins = v.memset(io["epsln"][:, :], EPS_LN)
        inc(ins, "sv", ("memset",))
        for tc in range(NT):
            wge(v, XSEM[tc % 3], ("x", tc))
            ins = E.x()
            if emit:
                sx = xstg(tc)
                ins = v.tensor_reduce(io["lnst"][:, 0:1], sx, AX.X, ALU.add)
            inc(ins, "sv", ("r1", tc))
            wge(v, "sv", ("r1", tc))
            ins = E.x()
            if emit:
                ins = v.tensor_scalar(io["lnst"][:, 1:2], io["lnst"][:, 0:1],
                                      -1.0 / C, None, ALU.mult)
            inc(ins, "sv", ("nm", tc))
            wge(v, "sa", ("sq", tc))
            wge(v, "sv", ("nm", tc))
            ins = E.x()
            if emit:
                ins = v.tensor_tensor(io["lnst"][:, 3:4], io["lnst"][:, 1:2],
                                      io["lnst"][:, 1:2], ALU.mult)
            inc(ins, "sv", ("m2", tc))
            wge(v, "sv", ("m2", tc))
            ins = E.x()
            if emit:
                ins = v.scalar_tensor_tensor(io["lnst"][:, 4:5],
                                             io["lnst"][:, 2:3], 1.0 / C,
                                             io["lnst"][:, 3:4],
                                             ALU.mult, ALU.subtract)
            inc(ins, "sv", ("var", tc))
            wge(v, "sa", ("rstd", tc))
            ins = E.x()
            if emit:
                ins = v.tensor_tensor(io["lnst"][:, 7:8], io["lnst"][:, 1:2],
                                      io["lnst"][:, 6:7], ALU.mult)
            inc(ins, "sv", ("bb", tc))
            wge(v, "sv", ("bb", tc))
            ins = E.x()
            if emit:
                xn = io["attn_pre"][:, 0:2048]
                ins = v.tensor_scalar(xn, sx, io["lnst"][:, 6:7],
                                      io["lnst"][:, 7:8], ALU.mult, ALU.add)
            inc(ins, "sv", ("cast", tc))
            for r in range(2):
                wge(v, "spe", ("xtr", tc, r))
                ins = E.x()
                if emit:
                    dst = io["xnT"].ap().rearrange("p (cc t) -> p cc t", cc=NCC)
                    dst = dst[:, r * 8:(r + 1) * 8, tc * 128:(tc + 1) * 128]
                    src = io["ps_tb"][r].ap().rearrange("p (z t) -> p z t", z=8)
                    ins = v.tensor_copy(dst, src)
                tb_done(ins, r)
    def B0_tensor(t):
        wge(t, "dmisc", ("idb",))
        wge(t, "sv", ("memset",))
        for tc in range(NT):
            wge(t, "sv", ("cast", tc))
            for r in range(2):
                tb_wait(t, r, key=("xnt", tc, r))
                ins = E.x()
                for blk_i in range(8):
                    if emit:
                        cc = r * 8 + blk_i
                        src = io["attn_pre"][:, cc * 128:(cc + 1) * 128]
                        ins = t.transpose(
                            io["ps_tb"][r][:, blk_i * 128:(blk_i + 1) * 128],
                            src, io["idb"][:, :])
                inc(ins, "spe", ("xtr", tc, r))
            if tc < 4:
                if tc == 0:
                    wge(t, "dw0", ("kslab", 0))
                uwait(t, ("xnt", tc, 0))
                uwait(t, ("xnt", tc, 1))
                k = tc % 4
                rot_wait(t, k, key=("Kt", 0, tc))
                ins = E.x()
                for cc in range(NCC):
                    if emit:
                        lhsT = io["xnT"][:, cc * 1024 + tc * 128:
                                         cc * 1024 + (tc + 1) * 128]
                        rhs = wslab(0)[:, cc * 256:(cc + 1) * 256]
                        ins = t.matmul(io["ps_r"][k][:, 0:256], lhsT, rhs,
                                       start=(cc == 0), stop=(cc == NCC - 1))
                inc(ins, "spe", ("kproj", 0, tc))

    def wslab(i):
        return io["wsb"][:, (i % 2) * 4096:(i % 2) * 4096 + 4096]

    KSEM = ["dw0", "dw1", "dk2"]

    def kwslab(i):
        if i % 3 < 2:
            return io["wsb"][:, (i % 3) * 4096:(i % 3) * 4096 + 4096]
        return io["attn_pre"][:, 2048:6144]

    def B1_gp(gp):
        for osl in range(8):
            if osl >= 3:
                wge(gp, "spe", ("kproj", osl - 3, NT - 1))
            ins = E.x()
            if emit:
                ins = gp.dma_start(out=kwslab(osl), in_=io["dwk"][osl])
            inc(ins, KSEM[osl % 3], ("kslab", osl), 16)
        for osl in range(8):
            if osl <= 1:
                wge(gp, "spe", ("kproj", 6 + osl, NT - 1))
            else:
                wge(gp, "spe", ("vproj", osl - 2, NT - 1))
            ins = E.x()
            if emit:
                ins = gp.dma_start(out=wslab(osl), in_=io["dwv"][osl])
            inc(ins, f"dw{osl % 2}", ("vslab", osl), 16)
            if osl == 1:
                B0_gp_ccg(gp)

    def B1_tensor(t):
        for tc in range(4, NT):
            uwait(t, ("xnt", tc, 0))
            uwait(t, ("xnt", tc, 1))
            k = tc % 4
            rot_wait(t, k, key=("Kt", 0, tc))
            ins = E.x()
            for cc in range(NCC):
                if emit:
                    lhsT = io["xnT"][:, cc * 1024 + tc * 128:
                                     cc * 1024 + (tc + 1) * 128]
                    rhs = kwslab(0)[:, cc * 256:(cc + 1) * 256]
                    ins = t.matmul(io["ps_r"][k][:, 0:256], lhsT, rhs,
                                   start=(cc == 0), stop=(cc == NCC - 1))
            inc(ins, "spe", ("kproj", 0, tc))
        tix = NT
        for osl in range(1, 8):
            wge(t, KSEM[osl % 3], ("kslab", osl))
            for tc in range(NT):
                k = tix % 4
                rot_wait(t, k, key=("Kt", osl, tc))
                ins = E.x()
                for cc in range(NCC):
                    if emit:
                        lhsT = io["xnT"][:, cc * 1024 + tc * 128:
                                         cc * 1024 + (tc + 1) * 128]
                        rhs = kwslab(osl)[:, cc * 256:(cc + 1) * 256]
                        ins = t.matmul(io["ps_r"][k][:, 0:256], lhsT, rhs,
                                       start=(cc == 0), stop=(cc == NCC - 1))
                inc(ins, "spe", ("kproj", osl, tc))
                tix += 1
        for osl in range(8):
            wge(t, f"dw{osl % 2}", ("vslab", osl))
            for tc in range(NT):
                k = tix % 4
                rot_wait(t, k, key=("Vt", osl, tc))
                ins = E.x()
                for cc in range(NCC):
                    if emit:
                        lhsT = io["xnT"][:, cc * 1024 + tc * 128:
                                         cc * 1024 + (tc + 1) * 128]
                        rhs = wslab(osl)[:, cc * 256:(cc + 1) * 256]
                        ins = t.matmul(io["ps_r"][k][:, 0:256], lhsT, rhs,
                                       start=(cc == 0), stop=(cc == NCC - 1))
                inc(ins, "spe", ("vproj", osl, tc))
                tix += 1
            if osl == 0:
                wge(t, "sv", ("gmat",))
                rot_wait(t, 0, key=("gmm_ev",))
                inc(t.matmul(io["ps_r"][0][0:1, 0:4], io["onecf"][:, 0:1],
                             io["gmat"][:, 0:4], start=True, stop=True)
                    if emit else E.x(), "spe", ("gmm",))
            uwait(t, ("Vt", osl, NT - 1))
            uwait(t, ("Kt", osl, NT - 1))
            g = osl % 2
            kb_wait(t, g, key=("kvg", osl))
            ins = E.x()
            first = True
            for hl in range(2):
                for tc in range(NT):
                    if emit:
                        base = KBUF + tc * 2048 + osl * 256 + hl * 128
                        kv_lhsT = io["arena"][:, base:base + 128]
                        vb = VST + g * 2048 + tc * 256 + hl * 128
                        vs = io["arena"][:, vb:vb + 128]
                        t.matmul(io["ps_kv"][g][:, hl * 128:(hl + 1) * 128],
                                 kv_lhsT, vs, start=first, stop=False)
                        ins = t.matmul(io["ps_kv"][g][:, 256 + hl:257 + hl],
                                       kv_lhsT, io["onecb"][:, 0:1],
                                       start=False,
                                       stop=(hl == 1 and tc == NT - 1))
                    first = False
            inc(ins, "spe", ("kvmm", osl))

    def B1_scalar(s):
        for osl in range(8):
            for tc in range(NT):
                i = osl * NT + tc
                wge(s, "spe", ("kproj", osl, tc))
                if i >= 2:
                    io2 = ((i - 2) // NT, (i - 2) % NT)
                    wge(s, "sv", ("kmn", io2[0], io2[1]))
                k = i % 4
                ins = E.x()
                if emit:
                    e = io["ev1"][:, (i % 2) * 256:(i % 2) * 256 + 256]
                    ins = s.activation(e, io["ps_r"][k][:, 0:256], AF.Exp)
                inc(ins, "sa", ("kexp", osl, tc))
        for osl in range(8):
            for tc in range(NT):
                wge(s, "spe", ("vproj", osl, tc))
                k = (64 + osl * NT + tc) % 4
                ins = E.x()
                if emit:
                    vb = VST + (osl % 2) * 2048 + tc * 256
                    ins = s.copy(io["arena"][:, vb:vb + 256],
                                 io["ps_r"][k][:, 0:256])
                rot_done(ins, k)

    def B1_vector(v):
        for osl in range(8):
            for tc in range(NT):
                i = osl * NT + tc
                wge(v, "sa", ("kexp", osl, tc))
                k = i % 4
                ins = E.x()
                if emit:
                    e = io["ev1"][:, (i % 2) * 256:(i % 2) * 256 + 256]
                    t1 = io["ev1"][:, 512 + (i % 2) * 256:512 + (i % 2) * 256 + 256]
                    ins = v.tensor_single_scalar(t1, e, 1.0, ALU.min)
                inc(ins, "sv", ("kmn", osl, tc))
                wge(v, "sv", ("kmn", osl, tc))
                ins = E.x()
                if emit:
                    base = KBUF + tc * 2048 + osl * 256
                    ins = v.scalar_tensor_tensor(io["arena"][:, base:base + 256],
                                                 io["ps_r"][k][:, 0:256], 0.0,
                                                 t1, ALU.max, ALU.add)
                rot_done(ins, k)
            for kk in range(3):
                k2 = osl * 3 + kk
                wge(v, f"dg{k2 % 2}", ("gpiece", k2))
                ins = E.x()
                if emit:
                    sg = io["wf32"][:, (k2 % 2) * 2048:(k2 % 2) * 2048 + GP]
                    ins = v.tensor_reduce(io["gacc"][:, k2:k2 + 1], sg, AX.X,
                                          ALU.add, apply_absolute_value=True)
                inc(ins, "sv", ("gred", k2))
        wge(v, "sv", ("gred", 3 * NGP - 1))
        ins = E.x()
        if emit:
            for m in range(3):
                ins = v.tensor_reduce(io["gmat"][:, m:m + 1],
                                      io["gacc"][:, m * NGP:(m + 1) * NGP],
                                      AX.X, ALU.add)
        inc(ins, "sv", ("gmat",))
        wge(v, "spe", ("gmm",))
        rot_done(v.tensor_copy(io["tsb"][0:1, 0:4], io["ps_r"][0][0:1, 0:4])
                 if emit else E.x(), 0)
        for osl in range(8):
            wge(v, "spe", ("kvmm", osl))
            g = osl % 2
            ins = E.x()
            if emit:
                v.tensor_copy(io["wf32"][:, osl * 256:(osl + 1) * 256],
                              io["ps_kv"][g][:, 0:256])
                ins = v.tensor_copy(io["ksum_sb"][:, osl * 2:osl * 2 + 2],
                                    io["ps_kv"][g][:, 256:258])
            kb_done(ins, g)
        uwait(v, ("kvg", 7))
        inc(v.tensor_copy(io["wf32"][:, 2048:2064], io["ksum_sb"][:, :])
            if emit else E.x(), "sv", ("ksfix",))

    def B1_gp2(gp):
        wge(gp, "sv", ("ksfix",))
        inc(gp.dma_start(out=io["cck_in"][:, :], in_=io["wf32"][:, 0:2064])
            if emit else E.x(), "dgp", ("cck_in",), 16)
        wge(gp, "dgp", ("cck_in",))
        ins = E.x()
        if emit and USE_CC:
            ins = gp.collective_compute(
                "AllReduce", ALU.add, replica_groups=[list(range(NCORE))],
                ins=[io["cck_in"].ap()], outs=[io["cck_out"].ap()])
        elif emit:
            ins = gp.nop()
        inc(ins, "cc", ("ark",), 1)

    def wqslab(h):
        return io["wsb"][:, (h % 4) * 2048:(h % 4) * 2048 + 2048]

    def qsl(h, th):
        return io["arena"][:, h * 1024 + th * 512:h * 1024 + (th + 1) * 512]

    WQSEM = ["dw0", "dw1", "dq0", "dq1"]

    def B2_gpA(gp):
        for h in range(4):
            wge(gp, "spe", ("vproj", 7, NT - 1))
            ins = E.x()
            if emit:
                ins = gp.dma_start(out=wqslab(h), in_=io["dwq"][h])
            inc(ins, WQSEM[h % 4], ("wqslab", h), 16)

    def B2_gp(gp):
        for h in range(4, 6):
            wge(gp, "spe", ("qproj", h - 4, 1))
            ins = E.x()
            if emit:
                ins = gp.dma_start(out=wqslab(h), in_=io["dwq"][h])
            inc(ins, WQSEM[h % 4], ("wqslab", h), 16)
        wge(gp, "cc", ("arg",))
        inc(gp.dma_start(out=io["gsum"][:, :], in_=io["ccg_out"][:, :])
            if emit else E.x(), "dgp", ("gsum",), 16)
        for h in range(6, H):
            wge(gp, "spe", ("qproj", h - 4, 1))
            ins = E.x()
            if emit:
                ins = gp.dma_start(out=wqslab(h), in_=io["dwq"][h])
            inc(ins, WQSEM[h % 4], ("wqslab", h), 16)
        wge(gp, "cc", ("ark",))
        ins1 = E.x()
        ins2 = E.x()
        if emit:
            ins1 = gp.dma_start(out=io["arena"][:, KVBF:KVBF + 2048],
                                in_=io["cck_out"][:, 0:2048])
            ins2 = gp.dma_start(out=io["ksum_bf"][:, :],
                                in_=io["cck_out"][:, 2048:2064])
        inc(ins1, "dgp", None, 16)
        inc(ins2, "dgp", ("kvbf",), 16)

    def B2_tensor(t):
        for h in range(H):
            wge(t, WQSEM[h % 4], ("wqslab", h))
            for th in range(2):
                k = (h * 2 + th) % 4
                rot_wait(t, k, key=("Qt", h, th))
                ins = E.x()
                for cc in range(NCC):
                    if emit:
                        lhsT = wqslab(h)[:, cc * 128:(cc + 1) * 128]
                        rhs = io["xnT"][:, cc * 1024 + th * 512:
                                        cc * 1024 + (th + 1) * 512]
                        ins = t.matmul(io["ps_r"][k][:, :], lhsT, rhs,
                                       start=(cc == 0), stop=(cc == NCC - 1))
                inc(ins, "spe", ("qproj", h, th))
            if h == 8:
                wge(t, "sv", ("tcalc",))
                rot_wait(t, 0, key=("tbc_ev",))
                inc(t.matmul(io["ps_r"][0][:, 0:4], io["ones1f"][0:1, :],
                             io["tsb"][0:1, 0:4], start=True, stop=True)
                    if emit else E.x(), "spe", ("tbc",))
        wge(t, "dgp", ("kvbf",))
        for h in range(H):
            for th in range(2):
                uwait(t, ("Qt", h, th))
                kb_wait(t, th, key=("dent", h, th))
                ins = E.x()
                if emit:
                    ins = t.matmul(io["ps_kv"][th][0:1, :],
                                   io["ksum_bf"][:, h:h + 1], qsl(h, th),
                                   start=True, stop=True)
                inc(ins, "spe", ("den", h, th))
            for th in range(2):
                ka = (h * 2 + th) % 4
                rot_wait(t, ka, key=("att", h, th))
                ins = E.x()
                if emit:
                    ins = t.matmul(io["ps_r"][ka][:, :],
                                   io["arena"][:, KVBF + h * 128:
                                               KVBF + (h + 1) * 128],
                                   qsl(h, th), start=True, stop=True)
                inc(ins, "spe", ("attn", h, th))
            for th in range(2):
                wge(t, "sa", ("z", h, th))
                kz = (h * 2 + th + 2) % 4
                rot_wait(t, kz, key=("zbt", h, th))
                ins = E.x()
                if emit:
                    ins = t.matmul(io["ps_r"][kz][:, :], io["ones1f"][0:1, :],
                                   io["xsl"][0:1, th * 512:(th + 1) * 512],
                                   start=True, stop=True)
                inc(ins, "spe", ("zb", h, th))

    def B2_scalar(s):
        for h in range(H):
            for th in range(2):
                wge(s, "spe", ("qproj", h, th))
                if h >= 1:
                    wge(s, "sv", ("qmn", h - 1, th))
                k = (h * 2 + th) % 4
                ins = E.x()
                if emit:
                    e = io["ev1"][:, th * 512:(th + 1) * 512]
                    ins = s.activation(e, io["ps_r"][k][:, :], AF.Exp)
                inc(ins, "sa", ("qexp", h, th))
        for h in range(H):
            for th in range(2):
                uwait(s, ("dent", h, th))
                ins = E.x()
                if emit:
                    zs = io["xsl"][0:1, th * 512:(th + 1) * 512]
                    ins = s.activation(zs, zs, AF.Ln)
                inc(ins, "sa", ("lnz", h, th))
                wge(s, "sa", ("lnz", h, th))
                ins = E.x()
                if emit:
                    zs = io["xsl"][0:1, th * 512:(th + 1) * 512]
                    ins = s.activation(zs, zs, AF.Exp, scale=-1.0)
                inc(ins, "sa", ("z", h, th))
            for th in range(2):
                wge(s, "spe", ("attn", h, th))
                if h >= 1:
                    uwait(s, ("zbt", h - 1, th))
                ka = (h * 2 + th) % 4
                ins = E.x()
                if emit:
                    ins = s.copy(io["ev2"][:, th * 512:(th + 1) * 512],
                                 io["ps_r"][ka][:, :])
                rot_done(ins, ka)

    def B2_vector(v):
        for h in range(H):
            if h == 6:
                wge(v, "dgp", ("gsum",))
                inc(v.tensor_scalar(io["tsb"][0:1, 0:4], io["gsum"][0:1, 0:4],
                                    0.5 / NW, 0.5 * EPS_Q, ALU.mult, ALU.add)
                    if emit else E.x(), "sv", ("tcalc",))
            if h == 9:
                wge(v, "spe", ("tbc",))
                if emit:
                    inc(v.tensor_copy(io["tbc"][:, 0:4],
                                      io["ps_r"][0][:, 0:4]), "sv", ("tbcast",))
                    rot_done(v.tensor_scalar(io["tbc"][:, 4:8],
                                             io["ps_r"][0][:, 0:4],
                                             -1.0, None, ALU.mult), 0)
                else:
                    inc(E.x(), "sv", ("tbcast",))
                    rot_done(E.x(), 0)
            for th in range(2):
                wge(v, "sa", ("qexp", h, th))
                k = (h * 2 + th) % 4
                ins = E.x()
                if emit:
                    e = io["ev1"][:, th * 512:(th + 1) * 512]
                    t1 = io["ev2"][:, th * 512:(th + 1) * 512]
                    ins = v.tensor_single_scalar(t1, e, 1.0, ALU.min)
                inc(ins, "sv", ("qmn", h, th))
                wge(v, "sv", ("qmn", h, th))
                ins = E.x()
                if emit:
                    t1 = io["ev2"][:, th * 512:(th + 1) * 512]
                    ins = v.scalar_tensor_tensor(qsl(h, th), io["ps_r"][k][:, :],
                                                 0.0, t1, ALU.max, ALU.add)
                rot_done(ins, k)
        for h in range(H):
            for th in range(2):
                wge(v, "spe", ("den", h, th))
                ins = E.x()
                if emit:
                    zs = io["xsl"][0:1, th * 512:(th + 1) * 512]
                    ins = v.tensor_single_scalar(zs, io["ps_kv"][th][0:1, :],
                                                 EPS_Z, ALU.add)
                kb_done(ins, th)
            for th in range(2):
                uwait(v, ("att", h, th))
                wge(v, "spe", ("zb", h, th))
                kz = (h * 2 + th + 2) % 4
                ins = E.x()
                if emit:
                    dst = io["attn_pre"][:, h * 1024 + th * 512:
                                         h * 1024 + (th + 1) * 512]
                    ins = v.tensor_tensor(dst,
                                          io["ev2"][:, th * 512:(th + 1) * 512],
                                          io["ps_r"][kz][:, :], ALU.mult)
                rot_done(ins, kz)

    def B3_sync(sp):
        wge(sp, "dgp", ("cck_in",))
        for j in range(NHC):
            for m in range(2):
                if j >= 1:
                    wge(sp, "sv", ("quant", j - 1, m))
                ins = E.x()
                if emit:
                    dw = io["dw1"] if m == 0 else io["dw2"]
                    dst = io["wf32"][:, m * 2048:(m + 1) * 2048]
                    ins = sp.dma_start(out=dst, in_=dw[j])
                inc(ins, f"ds{m}", ("wst", j, m), 16)

    def B3_vector(v):
        for j in range(NHC):
            for m in range(2):
                wge(v, f"ds{m}", ("wst", j, m))
                if j <= 1:
                    wge(v, "spe", ("qproj", H - 1, 1))
                if j >= 2:
                    wge(v, "spe", ("x12", j - 2, m, 1))
                ins = E.x()
                if emit:
                    wsrc = io["wf32"][:, m * 2048:(m + 1) * 2048]
                    qoff = (m * 2 + j % 2) * 2048
                    qdst = io["wsb"][:, qoff:qoff + 2048]
                    ins = v.tensor_scalar(qdst, wsrc, io["tbc"][:, 4 + m:5 + m],
                                          -1.0, ALU.is_lt, ALU.mult)
                inc(ins, "sv", ("q1", j, m))
                wge(v, "sv", ("q1", j, m))
                ins = E.x()
                if emit:
                    ins = v.scalar_tensor_tensor(qdst, wsrc,
                                                 io["tbc"][:, m:m + 1],
                                                 qdst, ALU.is_gt, ALU.add)
                inc(ins, "sv", ("quant", j, m))
            for th in range(2):
                uwait(v, ("x12t", j, 0, th))
                wge(v, "spe", ("x12", j, 1, th))
                k2 = (j * 4 + 2 + th) % 4
                ins = E.x()
                if emit:
                    s1 = io["ev1"][:, th * 512:(th + 1) * 512]
                    dst = io["arena"][:, j * 1024 + th * 512:
                                      j * 1024 + (th + 1) * 512]
                    ins = v.tensor_tensor(dst, s1, io["ps_r"][k2][:, :],
                                          ALU.mult)
                rot_done(ins, k2)

    def B3_scalar(s):
        for j in range(NHC):
            for th in range(2):
                wge(s, "spe", ("x12", j, 0, th))
                if j >= 1:
                    uwait(s, ("x12t", j - 1, 1, th))
                k1 = (j * 4 + th) % 4
                ins = E.x()
                if emit:
                    ins = s.activation(io["ev1"][:, th * 512:(th + 1) * 512],
                                       io["ps_r"][k1][:, :], AF.Sigmoid)
                rot_done(ins, k1)

    def B3_tensor(t):
        for j in range(NHC):
            for m in range(2):
                wge(t, "sv", ("quant", j, m))
                for th in range(2):
                    k = (j * 4 + m * 2 + th) % 4
                    rot_wait(t, k, key=("x12t", j, m, th))
                    ins = E.x()
                    for cc in range(NCC):
                        if emit:
                            qoff = (m * 2 + j % 2) * 2048
                            lhsT = io["wsb"][:, qoff + cc * 128:
                                             qoff + (cc + 1) * 128]
                            rhs = io["xnT"][:, cc * 1024 + th * 512:
                                            cc * 1024 + (th + 1) * 512]
                            ins = t.matmul(io["ps_r"][k][:, :], lhsT, rhs,
                                           start=(cc == 0),
                                           stop=(cc == NCC - 1))
                    inc(ins, "spe", ("x12", j, m, th))

    def B4_gp(gp):
        wge(gp, "spe", ("x12", NHC - 1, 1, 1))
        for oc in range(B4_NOC):
            if oc >= 2:
                wge(gp, "spe", ("wo_done", oc - 2))
            ins = E.x()
            if emit:
                dst = io["wsb"][:, (oc % 2) * 2048:(oc % 2 + 1) * 2048]
                ins = gp.dma_start(out=dst, in_=io["dwo"][oc])
            inc(ins, f"dw{oc % 2}", ("woslab", oc), 16)

    def B4_sync(sp):
        wge(sp, "sv", ("quant", NHC - 1, 1))
        wge(sp, "spe", ("zb", H - 1, 1))
        for oc in range(B4_NOC):
            if oc >= 2:
                wge(sp, f"do{oc % 2}", ("outdma", oc - 2))
            ins = E.x()
            if emit:
                dst = io["xsl"][:, (oc % 2) * 1024:(oc % 2 + 1) * 1024]
                ins = sp.dma_start(out=dst, in_=io["dxcs"][oc])
            inc(ins, f"dx{oc % 2}", ("xslab", oc), 16)
            for pi in range(4):
                gpi = oc * 4 + pi
                if gpi >= 2:
                    poc, ppi = divmod(gpi - 2, 4)
                    wge(sp, "sv", ("w3q", poc, ppi))
                ins = E.x()
                if emit:
                    dst = io["wf32"][:, (pi % 2) * 1408:(pi % 2) * 1408 + 1408]
                    ins = sp.dma_start(out=dst,
                                       in_=io["dw3"][oc][:, pi * 1408:(pi + 1) * 1408])
                inc(ins, f"ds{pi % 2}", ("w3st", oc, pi), 16)
            if oc >= 1:
                uwait(sp, ("otr", oc - 1, 0))
                uwait(sp, ("otr", oc - 1, 1))
                ins = E.x()
                if emit:
                    dst = io["dout"].rearrange("(tc p) c -> p tc c", p=128)
                    dst = dst[:, :, (oc - 1) * 128:oc * 128]
                    src = io["xsl"][:, ((oc - 1) % 2) * 1024:((oc - 1) % 2 + 1) * 1024]
                    src = src.rearrange("p (tc c) -> p tc c", tc=NT)
                    ins = sp.dma_start(out=dst, in_=src)
                inc(ins, f"do{(oc - 1) % 2}", ("outdma", oc - 1), 16)
        oc = B4_NOC - 1
        uwait(sp, ("otr", oc, 0))
        uwait(sp, ("otr", oc, 1))
        ins = E.x()
        if emit:
            dst = io["dout"].rearrange("(tc p) c -> p tc c", p=128)
            dst = dst[:, :, oc * 128:(oc + 1) * 128]
            src = io["xsl"][:, (oc % 2) * 1024:(oc % 2 + 1) * 1024]
            src = src.rearrange("p (tc c) -> p tc c", tc=NT)
            ins = sp.dma_start(out=dst, in_=src)
        inc(ins, f"do{oc % 2}", ("outdma", oc), 16)
        last_even = (B4_NOC - 1) // 2 * 2
        last_odd = (B4_NOC - 2) // 2 * 2 + 1
        wge(sp, "do0", ("outdma", last_even))
        if B4_NOC >= 2:
            wge(sp, "do1", ("outdma", last_odd))

    def B4_vector(v):
        for oc in range(B4_NOC):
            for pi in range(4):
                wge(v, f"ds{pi % 2}", ("w3st", oc, pi))
                if oc == 0 and pi == 0:
                    wge(v, "spe", ("x12", NHC - 1, 1, 1))
                if oc >= 2 and pi == 0:
                    wge(v, "spe", ("omm", oc - 2, 1))
                ins = E.x()
                if emit:
                    wsrc = io["wf32"][:, (pi % 2) * 1408:(pi % 2) * 1408 + 1408]
                    qdst = io["xnT"][:, (oc % 2) * 5632 + pi * 1408:
                                     (oc % 2) * 5632 + (pi + 1) * 1408]
                    ins = v.tensor_scalar(qdst, wsrc, io["tbc"][:, 6:7], -1.0,
                                          ALU.is_lt, ALU.mult)
                inc(ins, "sv", ("q31", oc, pi))
                wge(v, "sv", ("q31", oc, pi))
                ins = E.x()
                if emit:
                    wsrc = io["wf32"][:, (pi % 2) * 1408:(pi % 2) * 1408 + 1408]
                    qdst = io["xnT"][:, (oc % 2) * 5632 + pi * 1408:
                                     (oc % 2) * 5632 + (pi + 1) * 1408]
                    ins = v.scalar_tensor_tensor(qdst, wsrc, io["tbc"][:, 2:3],
                                                 qdst, ALU.is_gt, ALU.add)
                inc(ins, "sv", ("w3q", oc, pi))
            if oc >= 1:
                for th in range(2):
                    wge(v, "spe", ("otr", oc - 1, th))
                    wge(v, f"dx{(oc - 1) % 2}", ("xslab", oc - 1))
                    tb = th % 2
                    ins = E.x()
                    if emit:
                        xs = io["xsl"][:, ((oc - 1) % 2) * 1024 + th * 512:
                                       ((oc - 1) % 2) * 1024 + (th + 1) * 512]
                        ins = v.tensor_tensor(xs, io["ps_tb"][tb][:, 0:512],
                                              xs, ALU.add)
                    tb_done(ins, tb)
        oc = B4_NOC - 1
        for th in range(2):
            wge(v, "spe", ("otr", oc, th))
            wge(v, f"dx{oc % 2}", ("xslab", oc))
            tb = th % 2
            ins = E.x()
            if emit:
                xs = io["xsl"][:, (oc % 2) * 1024 + th * 512:
                               (oc % 2) * 1024 + (th + 1) * 512]
                ins = v.tensor_tensor(xs, io["ps_tb"][tb][:, 0:512],
                                      xs, ALU.add)
            tb_done(ins, tb)

    def B4_scalar(s):
        for oc in range(B4_NOC):
            for th in range(2):
                wge(s, "spe", ("omm", oc, th))
                if oc >= 1:
                    wge(s, "spe", ("otr", oc - 1, th))
                k = (oc * 2 + th) % 4
                ins = E.x()
                if emit:
                    ins = s.copy(io["ev2"][:, th * 512:(th + 1) * 512],
                                 io["ps_r"][k][:, :])
                rot_done(ins, k)

    def emit_tr(t, oc):
        for th in range(2):
            uwait(t, ("ot", oc, th))
            tb = th % 2
            tb_wait(t, tb, key=("otr", oc, th))
            ins = E.x()
            for blk_i in range(4):
                if emit:
                    src = io["ev2"][:, th * 512 + blk_i * 128:
                                    th * 512 + (blk_i + 1) * 128]
                    ins = t.transpose(
                        io["ps_tb"][tb][:, blk_i * 128:(blk_i + 1) * 128],
                        src, io["idb"][:, :])
            inc(ins, "spe", ("otr", oc, th))

    def B4_tensor(t):
        for oc in range(B4_NOC):
            wge(t, f"dw{oc % 2}", ("woslab", oc))
            ks = [(oc * 2) % 4, (oc * 2 + 1) % 4]
            for th in range(2):
                rot_wait(t, ks[th], key=("ot", oc, th))
            ins = E.x()
            for th in range(2):
                for cc in range(NCC):
                    if emit:
                        lhsT = io["wsb"][:, (oc % 2) * 2048 + cc * 128:
                                         (oc % 2) * 2048 + (cc + 1) * 128]
                        rhs = io["attn_pre"][:, cc * 1024 + th * 512:
                                             cc * 1024 + (th + 1) * 512]
                        ins = t.matmul(io["ps_r"][ks[th]][:, :], lhsT, rhs,
                                       start=(cc == 0), stop=False)
            inc(ins, "spe", ("wo_done", oc))
            if oc >= 1:
                emit_tr(t, oc - 1)
            wge(t, "sv", ("w3q", oc, 3))
            for th in range(2):
                ins = E.x()
                for hc in range(NHC):
                    if emit:
                        qoff = (oc % 2) * 5632 + hc * 128
                        lhsT = io["xnT"][:, qoff:qoff + 128]
                        rhs = io["arena"][:, hc * 1024 + th * 512:
                                          hc * 1024 + (th + 1) * 512]
                        ins = t.matmul(io["ps_r"][ks[th]][:, :], lhsT, rhs,
                                       start=False, stop=(hc == NHC - 1))
                inc(ins, "spe", ("omm", oc, th))
        emit_tr(t, B4_NOC - 1)

    nph = NPHASES
    if emit:
        if nph >= 4:
            with nc.Block("bmain") as blk:
                blk.sync(B0_sync)
                blk.tensor(B0_tensor)
                blk.tensor(B1_tensor)
                blk.tensor(B2_tensor)
                blk.tensor(B3_tensor)
                blk.tensor(B4_tensor)
                blk.gpsimd(B1_gp)
                blk.gpsimd(B2_gpA)
                blk.gpsimd(B1_gp2)
                blk.gpsimd(B2_gp)
                blk.gpsimd(B4_gp)
                blk.sync(B0_gamma_sync)
                blk.sync(B3_sync)
                blk.sync(B4_sync)
                blk.scalar(B0_scalar)
                blk.scalar(B1_scalar)
                blk.scalar(B2_scalar)
                blk.scalar(B3_scalar)
                blk.scalar(B4_scalar)
                blk.vector(B0_vector)
                blk.vector(B1_vector)
                blk.vector(B2_vector)
                blk.vector(B3_vector)
                blk.vector(B4_vector)

    else:
        if nph >= 5:
            B0_sync(E); B0_tensor(E); B1_tensor(E); B2_tensor(E); B3_tensor(E)
            B4_tensor(E)
            B1_gp(E); B2_gpA(E); B1_gp2(E); B2_gp(E); B4_gp(E)
            B0_gamma_sync(E); B3_sync(E); B4_sync(E)
            B0_scalar(E); B1_scalar(E); B2_scalar(E); B3_scalar(E); B4_scalar(E)
            B0_vector(E); B1_vector(E); B2_vector(E); B3_vector(E); B4_vector(E)
    return mil


def make_nc():
    mil = build_program(emit=False)
    nc = bass.Bass("TRN2", target_bir_lowering=False, debug=False)
    build_program(emit=True, nc=nc, mil=mil)
    return nc


def kernel(L0_attn_x, L0_attn_cos, L0_attn_sin, L0_attn_Wq, L0_attn_Wk,
           L0_attn_Wv, L0_attn_Wo, L0_ffn_w1, L0_ffn_w2, L0_ffn_w3):
    assert np.asarray(L0_attn_cos).shape == (4096, 128)
    assert np.asarray(L0_attn_sin).shape == (4096, 128)
    x = np.ascontiguousarray(
        np.asarray(L0_attn_x, dtype=np.float32).reshape(8192, C))
    wqT = np.asarray(L0_attn_Wq, dtype=np.float32).T
    wkT = np.asarray(L0_attn_Wk, dtype=np.float32).T
    wvT = np.asarray(L0_attn_Wv, dtype=np.float32).T
    woT = np.asarray(L0_attn_Wo, dtype=np.float32).T
    w1T = np.asarray(L0_ffn_w1, dtype=np.float32).T
    w2T = np.asarray(L0_ffn_w2, dtype=np.float32).T
    w3T = np.asarray(L0_ffn_w3, dtype=np.float32).T
    idb = np.eye(128).astype(ml_dtypes.bfloat16)

    def slabify(wt, nslab, width):
        kk = wt.shape[0] // 128
        a = wt.reshape(kk, 128, nslab, width).transpose(2, 1, 0, 3)
        return np.ascontiguousarray(a.reshape(nslab, 128, kk * width))

    wqS = slabify(wqT, H, 128)
    wkS = slabify(wkT, 8, 256)
    wvS = slabify(wvT, 8, 256)
    woS = slabify(woT, NCC, 128)
    w1S = slabify(w1T, NHC, 128)
    w2S = slabify(w2T, NHC, 128)
    w3S = slabify(w3T, NCC, 128)

    nr12 = C // NCORE
    nr3 = HF // NCORE
    in_maps = []
    for c in range(NCORE):
        g1 = np.ascontiguousarray(
            w1T[c * nr12:(c + 1) * nr12, :]).reshape(128, GCOLS)
        g2 = np.ascontiguousarray(
            w2T[c * nr12:(c + 1) * nr12, :]).reshape(128, GCOLS)
        g3 = np.ascontiguousarray(
            w3T[c * nr3:(c + 1) * nr3, :]).reshape(128, GCOLS)
        xc = np.ascontiguousarray(x[c * TC:(c + 1) * TC, :])
        xcs = np.ascontiguousarray(
            xc.reshape(NT, 128, NCC, 128).transpose(2, 1, 0, 3)).reshape(
                NCC, 128, NT * 128)
        in_maps.append({
            "x": xc, "xcs": xcs,
            "wqS": wqS, "wkS": wkS, "wvS": wvS, "woS": woS,
            "w1S": w1S, "w2S": w2S, "w3S": w3S,
            "g1": g1, "g2": g2, "g3": g3,
            "idb": idb,
        })

    nc = make_nc()
    res = run_bass_kernel_spmd(nc, in_maps, core_ids=list(range(NCORE)))
    out = np.concatenate([np.asarray(r["out"]) for r in res.results], axis=0)
    return out.reshape(1, 8192, C).astype(np.float32)
